# version 23
# baseline (speedup 1.0000x reference)
import numpy as np
import ml_dtypes
from contextlib import ExitStack

import concourse.bass as bass
import concourse.mybir as mybir
from concourse.bass_utils import run_bass_kernel_spmd

F32 = mybir.dt.float32
BF16 = mybir.dt.bfloat16
AF = mybir.ActivationFunctionType
ALU = mybir.AluOpType

NCORES = 8
NSEQ = 4
S = 2048
H = 1024
NH, HD = 16, 64
NL, LW = 16, 64
BIG = 32768.0

ENGINES = ("sync", "vector", "tensor", "scalar", "gpsimd")


class Prog:

    def __init__(self):
        self.items = {e: [] for e in ENGINES}
        self.counts = {}
        self.order = []

    def _sem(self, name):
        if name not in self.counts:
            self.counts[name] = 0
            self.order.append(name)

    def val(self, name):
        self._sem(name)
        return self.counts[name]

    def op(self, eng, fn, waits=(), inc=None, n=1):
        waits = tuple((s, v) for (s, v) in waits if v > 0)
        for s, v in waits:
            self._sem(s)
        if inc is not None:
            self._sem(inc)
            self.counts[inc] += n
        self.items[eng].append((waits, fn, inc, n))
        return (inc, self.counts[inc]) if inc is not None else None

    def barrier(self):
        self._sem("stg")
        target = self.counts["stg"] + len(ENGINES)
        for e in ENGINES:
            self.op(e, lambda eng: eng.nop(), inc="stg")
        for e in ENGINES:
            self.op(e, None, waits=(("stg", target),))
        return target

    def emit(self, nc, ctx):
        handles = {name: ctx.enter_context(nc.semaphore(name)) for name in self.order}
        blk = ctx.enter_context(nc.Block())

        def runner(engname):
            def body(eng):
                for waits, fn, inc, n in self.items[engname]:
                    for s, v in waits:
                        eng.wait_ge(handles[s], v)
                    if fn is None:
                        continue
                    ins = fn(eng)
                    if inc is not None:
                        ins.then_inc(handles[inc], n)
            return body

        blk.sync(runner("sync"))
        blk.vector(runner("vector"))
        blk.tensor(runner("tensor"))
        blk.scalar(runner("scalar"))
        blk.gpsimd(runner("gpsimd"))


def _leaf_matrices():
    E = np.zeros((16, 16), np.float32)
    for leaf in range(16):
        b = [(leaf >> 3) & 1, (leaf >> 2) & 1, (leaf >> 1) & 1, leaf & 1]
        nodes = [0, 1 + b[0], 3 + 2 * b[0] + b[1], 7 + 4 * b[0] + 2 * b[1] + b[2]]
        z = sum(1 for x in b if x == 0)
        for j, n in enumerate(nodes):
            E[n, leaf] += 2 * b[j] - 1.0
        E[15, leaf] += z
    D = BIG * E.copy()
    D[15, :] = BIG * (E[15, :] - 4.0)
    return E, D


def build(n_seq=NSEQ, dbg=False):
    nc = bass.Bass("TRN2", target_bir_lowering=False, debug=False)

    X = nc.dram_tensor("x", [n_seq, H, S], F32, kind="ExternalInput").ap()
    OUT = nc.dram_tensor("out", [n_seq, H, S], F32, kind="ExternalOutput").ap()
    win = {}
    for nm in ("wq", "wk", "wv", "wo", "w1q", "w2q", "w1k", "w2k", "w1v", "w2v"):
        win[nm] = nc.dram_tensor(nm, [H, H], F32, kind="ExternalInput").ap()
    B2 = {f: nc.dram_tensor("b2" + f, [16, H], F32, kind="ExternalInput").ap()
          for f in "qkv"}
    NWT = nc.dram_tensor("nwt", [H, 96], F32, kind="ExternalInput").ap()
    NODEBC = nc.dram_tensor("nodebc", [96, 1], F32, kind="ExternalInput").ap()
    BCOLS = nc.dram_tensor("bcols", [128, 48], F32, kind="ExternalInput").ap()
    BVR = nc.dram_tensor("bvr", [1, H], F32, kind="ExternalInput").ap()

    WS = nc.dram_tensor("ws", [10, 128, 8192], BF16, kind="Internal").ap()
    spk = "ExternalOutput" if dbg else "Internal"
    QPS = nc.dram_tensor("qps", [128, 16384], BF16, kind=spk).ap()
    KPS = nc.dram_tensor("kps", [128, 16384], BF16, kind=spk).ap()
    DBG = {}
    if dbg:
        for nm, shape, dt in (("d_xn", [128, 16384], BF16), ("d_g", [96, 2048], BF16),
                              ("d_sel", [80, 2048], BF16), ("d_q", [128, 16384], BF16),
                              ("d_k", [128, 16384], BF16), ("d_v", [128, 16640], BF16),
                              ("d_vp", [128, 16384], BF16), ("d_ctxn", [128, 16384], BF16),
                              ("d_murow", [1, 2048], F32), ("d_var", [1, 2048], F32),
                              ("d_rstd", [1, 2048], F32), ("d_mub", [128, 2048], F32),
                              ("d_rstdb", [128, 2048], F32)):
            DBG[nm] = nc.dram_tensor(nm, shape, dt, kind="ExternalOutput").ap()
    WIDX = {"wq": 0, "wk": 1, "wv": 2, "wo": 3, "w1q": 4, "w2q": 5,
            "w1k": 6, "w2k": 7, "w1v": 8, "w2v": 9}

    E, D = _leaf_matrices()
    cbig_np = np.repeat(D, LW, axis=1).astype(ml_dtypes.bfloat16)
    csmall_np = E.astype(ml_dtypes.bfloat16)
    CBIG = nc.inline_tensor(cbig_np, name="cbig").ap()
    CSMALL = nc.inline_tensor(csmall_np, name="csmall").ap()
    ONES128 = nc.inline_tensor(np.ones((1, 128), np.float32), name="ones128").ap()
    ONESCOLF = nc.inline_tensor(np.ones((128, 1), np.float32), name="onescolf").ap()
    ONESCOLB = nc.inline_tensor(np.ones((128, 1), ml_dtypes.bfloat16), name="onescolb").ap()

    with ExitStack() as ctx:
        at = lambda name, shape, dt, off: nc.alloc_sbuf_tensor_at(name, shape, dt, offset=off)

        A, B, C, Dr, Er, Fr = 16512, 82048, 115328, 148096, 180864, 214656
        xst = [at("xst0", [128, 2048], F32, A),
               at("xst1", [128, 2048], F32, A + 8192)]
        x2b = [at("x2b0", [128, 2048], F32, A + 16384),
               at("x2b1", [128, 2048], F32, A + 24576)]
        w1sb = [at("w1sb0", [128, 8192], BF16, A),
                at("w1sb1", [128, 8192], BF16, A + 16384)]
        w2sb = [at("w2sb0", [128, 8192], BF16, A + 32768),
                at("w2sb1", [128, 8192], BF16, A + 49152)]
        qk_t = [[at("q_t0", [128, 2048], BF16, A),
                 at("q_t1", [128, 2048], BF16, A + 4096)],
                [at("k_t0", [128, 2048], BF16, A + 8192),
                 at("k_t1", [128, 2048], BF16, A + 12288)]]
        ctxn = at("ctxn", [128, 16384], BF16, A + 32768)
        qp_sb = at("qp_sb", [128, 16384], BF16, A)
        kp_sb = at("kp_sb", [128, 16384], BF16, C)
        wst_f = [at("wstf0", [128, 1024], F32, Dr + 16384),
                 at("wstf1", [128, 1024], F32, Dr + 20480)]
        wst_b = [at("wstb0", [128, 1024], BF16, Dr + 24576),
                 at("wstb1", [128, 1024], BF16, Dr + 26624)]
        b2tmp = at("b2tmp", [16, 1024], F32, Er + 12288)
        bvtmp = at("bvtmp", [1, 1024], F32, Er + 16384)
        hid = [at("hid0", [128, 4096], BF16, B),
               at("hid1", [128, 4096], BF16, B + 8192)]
        spst = [at(f"spst{i}", [128, 512], BF16, B + 16384 + i * 1024) for i in range(4)]
        g_sb = at("g_sb", [96, 2048], BF16, B + 20480)
        selreg = at("selreg", [80, 2048], BF16, B + 24576)
        v_bf = at("v_bf", [128, 16640], BF16, B)
        pp_sb = at("pp_sb", [128, 16384], BF16, B)
        xn_bf = at("xn_bf", [128, 16384], BF16, C)
        mu_b = at("mu_b", [128, 2048], F32, Dr)
        rstd_b = at("rstd_b", [128, 2048], F32, Dr + 8192)
        vp_bf = at("vp_bf", [128, 16384], BF16, Dr)
        mu_row = at("mu_row", [1, 2048], F32, Er)
        var_row = at("var_row", [1, 2048], F32, Er + 8192)
        rstd_row = at("rstd_row", [1, 2048], F32, Er + 16384)
        wqs = at("wqs", [128, 1024], BF16, Er)
        wks = at("wks", [128, 1024], BF16, Er + 2048)
        wvs = at("wvs", [128, 1024], BF16, Er + 4096)
        p_sb = [at(f"psb{i}", [128, 1024], BF16, Er + 6144 + 2048 * i) for i in range(3)]
        sums_row = at("sums_row", [1, 1024], F32, Er + 12288)
        wstream = [None,
                   at("wstrB", [128, 8192], BF16, Er + 16384)]
        out_sb = [at("outsb0", [128, 512], F32, Er),
                  at("outsb1", [128, 512], F32, Er + 2048)]
        x_tile = [at("xt0", [128, 512], F32, Er + 4096),
                  at("xt1", [128, 512], F32, Er + 6144)]
        rcp2a = at("rcp2a", [1, 1024], F32, Er + 8192)
        rcp2b = at("rcp2b", [1, 1024], F32, Er + 12288)
        bcols_sb = at("bcols_sb", [128, 48], F32, Fr)
        nodebc_sb = at("nodebc_sb", [96, 1], F32, Fr + 192)
        ones128_sb = at("ones128_sb", [1, 128], F32, Fr + 224)
        onescolf_sb = at("onescolf_sb", [128, 1], F32, Fr + 736)
        onescolb_sb = at("onescolb_sb", [128, 1], BF16, Fr + 768)
        cbig_sb = at("cbig_sb", [80, 1024], BF16, Fr + 800)
        csmall_sb = at("csmall_sb", [80, 16], BF16, Fr + 2848)
        nwt_sb = at("nwt_sb", [128, 768], F32, Fr + 2880)
        b2rep = at("b2rep", [80, 1024], BF16, Fr + 5952)
        bv_bcast = at("bv_bcast", [128, 1024], F32, Fr + 8000)

        ps0 = ctx.enter_context(nc.psum_tensor("ps0", [128, 2048], F32))
        ps1 = ctx.enter_context(nc.psum_tensor("ps1", [128, 2048], F32))

        p = Prog()
        PE, DV, AC = "pe", "dv", "ac"
        dma_sems = set()

        def _dma(eng, sem_name, out, in_, waits=()):
            dma_sems.add(sem_name)
            m = p.op(eng, lambda e, o=out, i=in_: e.dma_start(out=o, in_=i),
                     waits=waits, inc=sem_name, n=16)
            return m

        def sdma(sem_name, out, in_, waits=()):
            return _dma("sync", sem_name, out, in_, waits)

        def pdma(sem_name, out, in_, waits=()):
            return _dma("gpsimd", sem_name, out, in_, waits)

        def tap(name, src):
            if not dbg:
                return
            pdma("tapsem", DBG[name], src)
            t = p.val("tapsem")
            for e in ENGINES:
                p.op(e, None, waits=(("tapsem", t),))

        _orig_op = p.op

        def _checked_op(eng, fn, waits=(), inc=None, n=1):
            for s, v in waits:
                if s in dma_sems and v > 0:
                    assert v == p.counts.get(s, 0), \
                        f"unsound DMA wait: {s} at {v} != full count {p.counts.get(s, 0)}"
            return _orig_op(eng, fn, waits=waits, inc=inc, n=n)

        p.op = _checked_op

        sdma("cst", nwt_sb[:, 0:768].rearrange("p (c n) -> p c n", c=8),
             NWT.rearrange("(c p) n -> p c n", p=128))
        sdma("cst", bcols_sb[:, :], BCOLS)
        sdma("cst", nodebc_sb[:, :], NODEBC)
        sdma("cst", ones128_sb[:, :], ONES128)
        sdma("cst", onescolf_sb[:, :], ONESCOLF)
        sdma("cst", onescolb_sb[:, :], ONESCOLB)
        for fi in range(3):
            sdma("cst", cbig_sb[fi * 32:fi * 32 + 16, :], CBIG)
            sdma("cst", csmall_sb[fi * 32:fi * 32 + 16, :], CSMALL)
        sdma("cst", bvtmp[:, :], BVR)
        consts_m = ("cst", p.val("cst"))

        wst_done = {}
        wfs_done = None
        prev_b2 = None
        for fi, f in enumerate("qkv"):
            lm = sdma("b2l", b2tmp[:, :], B2[f], waits=(prev_b2,) if prev_b2 else ())
            cm = p.op("vector",
                      lambda e, fi=fi: e.tensor_copy(b2rep[fi * 32:fi * 32 + 16, :],
                                                     b2tmp[:, :]),
                      waits=(lm,), inc=DV)
            prev_b2 = (DV, cm[1])
        for ncc in range(2):
            p.op("tensor",
                 lambda e, ncc=ncc: e.matmul(ps0[:, ncc * 512:(ncc + 1) * 512],
                                             ones128_sb[0:1, :],
                                             bvtmp[0:1, ncc * 512:(ncc + 1) * 512],
                                             start=True, stop=True),
                 waits=((consts_m,) if ncc == 0 else ()), inc=PE)
        bv_pe = p.val(PE)
        p.op("scalar", lambda e: e.copy(bv_bcast[:, :], ps0[:, 0:1024]),
             waits=((PE, bv_pe),), inc=AC)
        p.op("sync", None, waits=(consts_m, ("b2l", p.val("b2l"))))
        p.barrier()

        qk_evac_bias = {"q": bcols_sb[:, 0:8], "k": bcols_sb[:, 8:16]}
        b1c = {"q": bcols_sb[:, 24:32], "k": bcols_sb[:, 32:40], "v": bcols_sb[:, 40:48]}

        for seq in range(n_seq):
            xw = [None, None]
            xwd = [None, None]
            x2w = [None, None]
            for c in range(8):
                j = c % 2
                lm = sdma(f"xl{j}", xst[j][:, :], X[seq, c * 128:(c + 1) * 128, :],
                          waits=((xw[j],) if xw[j] else ()) +
                                ((xwd[j],) if xwd[j] else ()))
                sq = p.op("vector",
                          lambda e, j=j: e.tensor_mul(x2b[j][:, :], xst[j][:, :],
                                                      xst[j][:, :]),
                          waits=(lm,) + ((x2w[j],) if x2w[j] else ()), inc=DV)
                xwd[j] = (DV, sq[1])
                for ncc in range(4):
                    p.op("tensor",
                         lambda e, j=j, ncc=ncc, c=c: e.matmul(
                             ps0[0:1, ncc * 512:(ncc + 1) * 512], onescolf_sb[:, :],
                             xst[j][:, ncc * 512:(ncc + 1) * 512],
                             start=(c == 0), stop=(c == 7)),
                         waits=((lm,) if ncc == 0 else ()),
                         inc=PE if ncc == 3 else None)
                xw[j] = (PE, p.val(PE))
                for ncc in range(4):
                    p.op("tensor",
                         lambda e, j=j, ncc=ncc, c=c: e.matmul(
                             ps1[0:1, ncc * 512:(ncc + 1) * 512], onescolf_sb[:, :],
                             x2b[j][:, ncc * 512:(ncc + 1) * 512],
                             start=(c == 0), stop=(c == 7)),
                         waits=(((DV, sq[1]),) if ncc == 0 else ()),
                         inc=PE if ncc == 3 else None)
                x2w[j] = (PE, p.val(PE))
            p1_pe = p.val(PE)
            mu_m = p.op("vector",
                        lambda e: e.tensor_scalar(mu_row[:, :], ps0[0:1, 0:2048],
                                                  1.0 / H, None, ALU.mult),
                        waits=((PE, p1_pe),), inc=DV)
            for ncc in range(4):
                p.op("tensor",
                     lambda e, ncc=ncc: e.matmul(ps0[:, ncc * 512:(ncc + 1) * 512],
                                                 ones128_sb[0:1, :],
                                                 mu_row[0:1, ncc * 512:(ncc + 1) * 512],
                                                 start=True, stop=True),
                     waits=(((DV, mu_m[1]),) if ncc == 0 else ()),
                     inc=PE if ncc == 3 else None)
            mub_pe = p.val(PE)
            mub_m = p.op("scalar", lambda e: e.copy(mu_b[:, :], ps0[:, 0:2048]),
                         waits=((PE, mub_pe),), inc=AC)
            mu2_m = p.op("vector",
                         lambda e: e.tensor_mul(rstd_row[:, :], mu_row[:, :], mu_row[:, :]),
                         waits=((PE, mub_pe),), inc=DV)
            var_m = p.op("vector",
                         lambda e: e.scalar_tensor_tensor(var_row[:, :], ps1[0:1, 0:2048],
                                                          1.0 / H, rstd_row[:, :],
                                                          ALU.mult, ALU.subtract),
                         inc=DV)
            vare_m = p.op("vector",
                          lambda e: e.tensor_scalar(var_row[:, :], var_row[:, :],
                                                    1e-5, None, ALU.add),
                          inc=DV)
            ln_m = p.op("scalar",
                        lambda e: e.activation(mu_row[:, :], var_row[:, :], AF.Ln),
                        waits=((DV, vare_m[1]),), inc=AC)
            rst_m = p.op("scalar",
                         lambda e: e.activation(rstd_row[:, :], mu_row[:, :], AF.Exp,
                                                scale=-0.5),
                         waits=((DV, mu2_m[1]),), inc=AC)
            for ncc in range(4):
                p.op("tensor",
                     lambda e, ncc=ncc: e.matmul(ps0[:, ncc * 512:(ncc + 1) * 512],
                                                 ones128_sb[0:1, :],
                                                 rstd_row[0:1, ncc * 512:(ncc + 1) * 512],
                                                 start=True, stop=True),
                     waits=(((AC, rst_m[1]), (AC, mub_m[1])) if ncc == 0 else ()),
                     inc=PE if ncc == 3 else None)
            rb_pe = p.val(PE)
            rb_m = p.op("scalar", lambda e: e.copy(rstd_b[:, :], ps0[:, 0:2048]),
                        waits=((PE, rb_pe),), inc=AC)
            if seq > 0:
                for wi, (nm, dst) in enumerate((("wq", wqs), ("wk", wks), ("wv", wvs))):
                    sdma(f"w{nm[1]}s",
                         dst[:, :].rearrange("p (kc n) -> p kc n", kc=8),
                         WS[WIDX[nm], :, :].rearrange("p (kc m n) -> p kc m n", kc=8, m=8)
                         [:, :, 0, :],
                         waits=((AC, rst_m[1]),) if wi == 0 else ())
                ldm_pair0 = (("wqs", p.val("wqs")), ("wks", p.val("wks")),
                             ("wvs", p.val("wvs")))
            bufs4 = [xst[0], xst[1], x2b[0], x2b[1]]
            pxw = [xw[0], xw[1], x2w[0], x2w[1]]
            pxd = [xwd[0], xwd[1], xwd[0], xwd[1]]
            for c in range(8):
                b = c % 4
                lm = sdma(f"xl{b}", bufs4[b][:, :], X[seq, c * 128:(c + 1) * 128, :],
                          waits=((pxw[b],) if pxw[b] else ()) +
                                ((pxd[b],) if pxd[b] else ()))
                s1 = p.op("vector",
                          lambda e, b=b: e.tensor_sub(bufs4[b][:, :], bufs4[b][:, :],
                                                      mu_b[:, :]),
                          waits=(lm, (AC, rb_m[1])), inc=DV)
                s2 = p.op("vector",
                          lambda e, b=b: e.tensor_mul(bufs4[b][:, :], bufs4[b][:, :],
                                                      rstd_b[:, :]),
                          inc=DV)
                ct = p.op("scalar",
                          lambda e, b=b, c=c: e.copy(
                              xn_bf[:, c * 2048:(c + 1) * 2048], bufs4[b][:, :]),
                          waits=((DV, s2[1]),), inc=AC)
                pxd[b] = (AC, ct[1])
                for ncc in range(4):
                    p.op("tensor",
                         lambda e, b=b, ncc=ncc, c=c: e.matmul(
                             ps1[0:96, ncc * 512:(ncc + 1) * 512],
                             nwt_sb[:, c * 96:(c + 1) * 96],
                             bufs4[b][:, ncc * 512:(ncc + 1) * 512],
                             start=(c == 0), stop=(c == 7)),
                         waits=(((DV, s2[1]),) if ncc == 0 else ()),
                         inc=PE if ncc == 3 else None)
                pxw[b] = (PE, p.val(PE))
            node_pe = p.val(PE)
            g_m = p.op("vector",
                       lambda e: e.tensor_scalar(g_sb[:, :], ps1[0:96, 0:2048],
                                                 nodebc_sb[:, 0:1], 0.0,
                                                 ALU.add, ALU.is_ge),
                       waits=((PE, node_pe),), inc=DV)
            sel_dv = None
            for fi in range(3):
                for ncc in range(4):
                    p.op("tensor",
                         lambda e, fi=fi, ncc=ncc: e.matmul(
                             ps1[0:16, ncc * 512:(ncc + 1) * 512],
                             csmall_sb[fi * 32:fi * 32 + 16, :],
                             g_sb[fi * 32:fi * 32 + 16, ncc * 512:(ncc + 1) * 512],
                             start=True, stop=True),
                         waits=((DV, g_m[1]),) + (((DV, sel_dv[1]),) if sel_dv else ()),
                         inc=PE if ncc == 3 else None)
                sp = p.val(PE)
                sel_dv = p.op("vector",
                              lambda e, fi=fi: e.tensor_scalar(
                                  selreg[fi * 32:fi * 32 + 16, :], ps1[0:16, 0:2048],
                                  3.5, None, ALU.is_ge),
                              waits=((PE, sp),), inc=DV)
            p.barrier()
            if seq == 0:
                tap("d_xn", xn_bf[:, :])
                tap("d_g", g_sb[:, :])
                tap("d_sel", selreg[:, :])
                tap("d_murow", mu_row[:, :])
                tap("d_var", var_row[:, :])
                tap("d_rstd", rstd_row[:, :])
                tap("d_mub", mu_b[:, :])
                tap("d_rstdb", rstd_b[:, :])

            if seq == 0:
                war = {}
                for oi, nm in enumerate(("w1q", "w2q", "w1k", "w2k", "w1v",
                                         "w2v", "wq", "wk", "wv", "wo")):
                    for c in range(8):
                        j = (oi * 8 + c) % 2
                        lm = _dma("scalar", f"wfl{j}", wst_f[j][:, :],
                                  win[nm][c * 128:(c + 1) * 128, :])
                        cm = p.op("scalar",
                                  lambda e, j=j: e.copy(wst_b[j][:, :],
                                                        wst_f[j][:, :]),
                                  waits=(lm,) + ((war[("b", j)],) if ("b", j) in war else ()),
                                  inc=AC)
                        sm = _dma("scalar", f"wf{j}_{nm}",
                                  WS[WIDX[nm], :, c * 1024:(c + 1) * 1024],
                                  wst_b[j][:, :])
                        war[("b", j)] = sm
                    wst_done[nm] = ((f"wf0_{nm}", p.val(f"wf0_{nm}")),
                                    (f"wf1_{nm}", p.val(f"wf1_{nm}")))
                wfs_done = wst_done["wo"]

            wl_m = {}

            def emit_fff_loads(fi, waits=()):
                f = "qkv"[fi]
                j = fi % 2
                if seq == 0:
                    waits = tuple(waits) + wst_done["w2" + f]
                for c in range(8):
                    sdma(f"w1l{j}", w1sb[j][:, c * 1024:(c + 1) * 1024],
                         WS[WIDX["w1" + f], :, c * 1024:(c + 1) * 1024],
                         waits=waits if c == 0 else ())
                    sdma(f"w2l{j}", w2sb[j][:, c * 1024:(c + 1) * 1024],
                         WS[WIDX["w2" + f], :, c * 1024:(c + 1) * 1024])
                wl_m[("ld", fi)] = ((f"w1l{j}", p.val(f"w1l{j}")),
                                    (f"w2l{j}", p.val(f"w2l{j}")))

            emit_fff_loads(0)
            emit_fff_loads(1)
            hid_war = [None, None]
            psu_war = [None] * 4
            psw_war = [None] * 4
            spst_war = [None] * 4
            for fi, f in enumerate("qkv"):
                j = fi % 2
                lm = wl_m[("ld", fi)]
                for tc in range(4):
                    hb = tc % 2
                    ev_ms = []
                    for Mc in range(8):
                        u = (tc * 8 + Mc) % 4
                        for Kc in range(8):
                            p.op("tensor",
                                 lambda e, j=j, Mc=Mc, Kc=Kc, tc=tc, u=u: e.matmul(
                                     ps0[:, u * 512:(u + 1) * 512],
                                     w1sb[j][:, Kc * 1024 + Mc * 128: Kc * 1024 + Mc * 128 + 128],
                                     xn_bf[:, Kc * 2048 + tc * 512: Kc * 2048 + (tc + 1) * 512],
                                     start=(Kc == 0), stop=False),
                                 waits=(lm if (tc == 0 and Mc == 0 and Kc == 0) else ()) +
                                       ((psu_war[u],) if (psu_war[u] and Kc == 0) else ()))
                        pm = p.op("tensor",
                                  lambda e, fi=fi, Mc=Mc, tc=tc, u=u: e.matmul(
                                      ps0[:, u * 512:(u + 1) * 512],
                                      cbig_sb[fi * 32:fi * 32 + 16, Mc * 128:(Mc + 1) * 128],
                                      g_sb[fi * 32:fi * 32 + 16, tc * 512:(tc + 1) * 512],
                                      start=False, stop=True),
                                  inc=PE)
                        ev = p.op("vector",
                                  lambda e, f=f, Mc=Mc, hb=hb, u=u: e.tensor_scalar(
                                      hid[hb][:, Mc * 512:(Mc + 1) * 512],
                                      ps0[:, u * 512:(u + 1) * 512],
                                      b1c[f][:, Mc:Mc + 1], 0.0, ALU.add, ALU.max),
                                  waits=((PE, pm[1]),) +
                                        ((hid_war[hb],) if (hid_war[hb] and Mc == 0) else ()),
                                  inc=DV)
                        psu_war[u] = (DV, ev[1])
                        ev_ms.append(ev)
                    hid_done = ev_ms[-1]
                    if f in ("q", "k"):
                        for Oc in range(8):
                            u = (tc * 8 + Oc) % 4
                            for Kc in range(8):
                                p.op("tensor",
                                     lambda e, j=j, Oc=Oc, Kc=Kc, hb=hb, u=u: e.matmul(
                                         ps1[:, u * 512:(u + 1) * 512],
                                         w2sb[j][:, Kc * 1024 + Oc * 128: Kc * 1024 + Oc * 128 + 128],
                                         hid[hb][:, Kc * 512:(Kc + 1) * 512],
                                         start=(Kc == 0), stop=False),
                                     waits=((DV, ev_ms[Kc][1]),) +
                                           ((psw_war[u],) if (psw_war[u] and Kc == 0) else ()))
                            sm = p.op("tensor",
                                      lambda e, fi=fi, Oc=Oc, tc=tc, u=u: e.matmul(
                                          ps1[:, u * 512:(u + 1) * 512],
                                          b2rep[fi * 32:fi * 32 + 16, Oc * 128:(Oc + 1) * 128],
                                          selreg[fi * 32:fi * 32 + 16, tc * 512:(tc + 1) * 512],
                                          start=False, stop=True),
                                      inc=PE)
                            sbuf_i = (tc * 8 + Oc) % 4
                            cv = p.op("vector",
                                      lambda e, sbuf_i=sbuf_i, u=u: e.tensor_copy(
                                          spst[sbuf_i][:, :], ps1[:, u * 512:(u + 1) * 512]),
                                      waits=((PE, sm[1]),) +
                                            ((spst_war[sbuf_i],) if spst_war[sbuf_i] else ()),
                                      inc=DV)
                            psw_war[u] = (DV, cv[1])
                            dst = (QPS if f == "q" else KPS)
                            dm = pdma(f"sps{sbuf_i}",
                                      dst[:, Oc * 2048 + tc * 512: Oc * 2048 + (tc + 1) * 512],
                                      spst[sbuf_i][:, :], waits=(cv,))
                            spst_war[sbuf_i] = dm
                    else:
                        for ts4 in range(4):
                            t128 = tc * 512 + ts4 * 128
                            for ncc in range(2):
                                u = (tc * 8 + ts4 * 2 + ncc) % 4
                                for Kc in range(8):
                                    p.op("tensor",
                                         lambda e, j=j, Kc=Kc, hb=hb, ts4=ts4, ncc=ncc, u=u: e.matmul(
                                             ps1[:, u * 512:(u + 1) * 512],
                                             hid[hb][:, Kc * 512 + ts4 * 128: Kc * 512 + ts4 * 128 + 128],
                                             w2sb[j][:, Kc * 1024 + ncc * 512: Kc * 1024 + (ncc + 1) * 512],
                                             start=(Kc == 0), stop=False),
                                         waits=((DV, hid_done[1]),) +
                                               ((psw_war[u],) if (psw_war[u] and Kc == 0) else ()))
                                sm = p.op("tensor",
                                          lambda e, t128=t128, ncc=ncc, u=u: e.matmul(
                                              ps1[:, u * 512:(u + 1) * 512],
                                              selreg[64:80, t128:t128 + 128],
                                              b2rep[64:80, ncc * 512:(ncc + 1) * 512],
                                              start=False, stop=True),
                                          inc=PE)
                                cv = p.op("vector",
                                          lambda e, tc=tc, ts4=ts4, ncc=ncc, u=u: e.tensor_copy(
                                              vp_bf[:, (tc * 4 + ts4) * 1024 + ncc * 512:
                                                    (tc * 4 + ts4) * 1024 + (ncc + 1) * 512],
                                              ps1[:, u * 512:(u + 1) * 512]),
                                          waits=((PE, sm[1]),) +
                                                (wfs_done if (seq == 0 and tc == 0 and
                                                              ts4 == 0 and ncc == 0) else ()),
                                          inc=DV)
                                psw_war[u] = (DV, cv[1])
                    hid_war[hb] = (PE, p.val(PE))
                wl_m[("done", fi)] = (PE, p.val(PE))
                if fi + 2 < 3:
                    emit_fff_loads(fi + 2, waits=(wl_m[("done", fi)],))
            spill_done = tuple((f"sps{i}", p.val(f"sps{i}")) for i in range(4))
            p.barrier()
            if seq == 0:
                tap("d_vp", vp_bf[:, :])

            vones = p.op("vector",
                         lambda e: e.memset(
                             v_bf[:, :].rearrange("p (c h w) -> p c h w", c=16, h=16)
                             [:, :, :, 64:65], 1.0), inc=DV)
            for c in range(8):
                sdma("wsl1", wstream[1][:, c * 1024:(c + 1) * 1024],
                     WS[WIDX["wo"], :, c * 1024:(c + 1) * 1024],
                     waits=wfs_done if (seq == 0 and c == 0) else ())
            wsm_wo = ("wsl1", p.val("wsl1"))
            proj_pe_last = [None]
            wsl_war = [None]
            mwar = [None, None]
            qk_war = [[None, None], [None, None]]
            vv_war = [None, None]

            def emit_pair_loads(i, buf):
                w = (wsl_war[0],) if wsl_war[0] else ()
                for wi, (nm, dst) in enumerate((("wq", wqs), ("wk", wks), ("wv", wvs))):
                    sdma(f"w{nm[1]}s",
                         dst[:, :].rearrange("p (kc n) -> p kc n", kc=8),
                         WS[WIDX[nm], :, :].rearrange("p (kc m n) -> p kc m n", kc=8, m=8)
                         [:, :, i, :],
                         waits=w if wi == 0 else ())
                return (("wqs", p.val("wqs")), ("wks", p.val("wks")),
                        ("wvs", p.val("wvs")))

            def mk_proj_groups(i, buf, ldm):
                atoms = []
                for part in range(2):
                    wsl = (wqs, wks)[part]
                    bias = qk_evac_bias["qk"[part]][:, i:i + 1]
                    for ncc in range(4):
                        for Kc in range(8):
                            def g(part=part, wsl=wsl, bias=bias, ncc=ncc, i=i,
                                  buf=buf, ldm=ldm, Kc=Kc):
                                un = ncc % 2
                                p.op("tensor",
                                     lambda e, wsl=wsl, Kc=Kc, ncc=ncc, un=un: e.matmul(
                                         ps1[:, 1024 + un * 512: 1024 + (un + 1) * 512],
                                         wsl[:, Kc * 128:(Kc + 1) * 128],
                                         xn_bf[:, Kc * 2048 + ncc * 512: Kc * 2048 + (ncc + 1) * 512],
                                         start=(Kc == 0), stop=(Kc == 7)),
                                     waits=((ldm if (part == 0 and ncc == 0 and Kc == 0) else ()) +
                                            ((mwar[un],) if (mwar[un] and Kc == 0) else ())),
                                     inc=PE if Kc == 7 else None)
                                if Kc != 7:
                                    return
                                pm = p.val(PE)
                                proj_pe_last[0] = (PE, pm)
                                if part == 1 and ncc == 3:
                                    wsl_war[0] = (PE, pm)
                                ev = p.op("vector",
                                          lambda e, part=part, bias=bias, ncc=ncc, buf=buf, un=un: e.tensor_scalar(
                                              qk_t[part][buf][:, ncc * 512:(ncc + 1) * 512],
                                              ps1[:, 1024 + un * 512: 1024 + (un + 1) * 512],
                                              bias, None, ALU.add),
                                          waits=((PE, pm),) +
                                                ((qk_war[part][buf],) if (qk_war[part][buf] and ncc == 0) else ()),
                                          inc=DV)
                                mwar[un] = (DV, ev[1])
                                if ncc == 3:
                                    qk_war[part][buf] = (DV, ev[1])
                            atoms.append((g, Kc == 0))
                for tc in range(16):
                    for Kc in range(8):
                        def g(tc=tc, i=i, buf=buf, Kc=Kc):
                            un = tc % 2
                            p.op("tensor",
                                 lambda e, tc=tc, Kc=Kc, i=i, un=un: e.matmul(
                                     ps1[:, 1024 + un * 512: 1024 + un * 512 + 128],
                                     xn_bf[:, Kc * 2048 + tc * 128: Kc * 2048 + tc * 128 + 128],
                                     wvs[:, Kc * 128:(Kc + 1) * 128],
                                     start=(Kc == 0), stop=(Kc == 7)),
                                 waits=((mwar[un],) if (mwar[un] and Kc == 0) else ()),
                                 inc=PE if Kc == 7 else None)
                            if Kc != 7:
                                return
                            pm = p.val(PE)
                            proj_pe_last[0] = (PE, pm)
                            if tc == 15:
                                wsl_war[0] = (PE, pm)
                            ev = p.op("vector",
                                      lambda e, tc=tc, i=i, un=un: e.tensor_add(
                                          v_bf[:, tc * 1040 + (2 * i) * 65: tc * 1040 + (2 * i) * 65 + 130]
                                          .rearrange("p (h w) -> p h w", h=2)[:, :, 0:64],
                                          ps1[:, 1024 + un * 512: 1024 + un * 512 + 128]
                                          .rearrange("p (h w) -> p h w", h=2),
                                          bv_bcast[:, (2 * i) * 64: (2 * i) * 64 + 128]
                                          .rearrange("p (h w) -> p h w", h=2)),
                                      waits=((PE, pm), (DV, vones[1])), inc=DV)
                            mwar[un] = (DV, ev[1])
                            if tc == 15:
                                vv_war[buf] = (DV, ev[1])
                        atoms.append((g, Kc == 0))
                return atoms

            s_war = [None, None]
            psb_war = [None, None, None]
            ctx_war = [None, None]
            sums_war = [None]
            pending = []
            unit_no = [0]
            ppop = [0]
            pgoal = [24]
            if seq == 0:
                for wi, (nm, dst) in enumerate((("wq", wqs), ("wk", wks), ("wv", wvs))):
                    sdma(f"w{nm[1]}s",
                         dst[:, :].rearrange("p (kc n) -> p kc n", kc=8),
                         WS[WIDX[nm], :, :].rearrange("p (kc m n) -> p kc m n", kc=8, m=8)
                         [:, :, 0, :],
                         waits=wfs_done if wi == 0 else ())
                ldm_pair0 = (("wqs", p.val("wqs")), ("wks", p.val("wks")),
                             ("wvs", p.val("wvs")))
            proj_q = mk_proj_groups(0, 0, ldm_pair0)
            for g, _ in proj_q:
                g()
            ldm_next = emit_pair_loads(1, 1)
            proj_q = mk_proj_groups(1, 1, ldm_next)
            for i in range(NH // 2):
                buf = i % 2
                unit_no[0] = 0
                ppop[0] = 0
                pgoal[0] = len(proj_q)
                for qt in range(4):
                    exp_m = [None] * 16
                    pv_m = [None] * 16

                    def emit_pv(kc, i=i, qt=qt, buf=buf):
                        for px in range(2):
                            h = 2 * i + px
                            p.op("tensor",
                                 lambda e, h=h, px=px, kc=kc: e.matmul(
                                     ps1[0:65, px * 512:(px + 1) * 512],
                                     v_bf[:, kc * 1040 + h * 65: kc * 1040 + h * 65 + 65],
                                     p_sb[kc % 3][:, px * 512:(px + 1) * 512],
                                     start=(kc == 0), stop=(kc == 15)),
                                 waits=(((AC, exp_m[kc][1]),) if px == 0 else ()) +
                                       ((ctx_war[px],) if (ctx_war[px] and kc == 0) else ()),
                                 inc=PE if px == 1 else None)
                        pv_m[kc] = (PE, p.val(PE))
                        psb_war[kc % 3] = pv_m[kc]

                    for kc in range(16):
                        b = kc % 2
                        for px in range(2):
                            hp = px * 64
                            p.op("tensor",
                                 lambda e, hp=hp, px=px, qt=qt, kc=kc, buf=buf, b=b: e.matmul(
                                     ps0[:, b * 1024 + px * 512: b * 1024 + (px + 1) * 512],
                                     qk_t[1][buf][hp:hp + 64, kc * 128:(kc + 1) * 128],
                                     qk_t[0][buf][hp:hp + 64, qt * 512:(qt + 1) * 512],
                                     start=True, stop=True),
                                 waits=((s_war[b],) if (s_war[b] and px == 0) else ()) +
                                       ((qk_war[0][buf], qk_war[1][buf], vv_war[buf])
                                        if (qt == 0 and kc == 0 and px == 0) else ()),
                                 inc=PE if px == 1 else None)
                        sm = p.val(PE)
                        if kc >= 2:
                            emit_pv(kc - 2)
                        unit_no[0] += 1
                        if pending and (kc % 8) == 4:
                            while proj_q and not proj_q[0][1]:
                                proj_q.pop(0)[0]()
                                ppop[0] += 1
                            pending.pop(0)()
                        ptarget = ((unit_no[0] + 1) * pgoal[0]) // 64
                        while ppop[0] < ptarget and proj_q:
                            for _ in range(4):
                                if proj_q:
                                    proj_q.pop(0)[0]()
                                    ppop[0] += 1
                        em = p.op("scalar",
                                  lambda e, b=b, kc=kc: e.activation(
                                      p_sb[kc % 3][:, :], ps0[:, b * 1024:(b + 1) * 1024],
                                      AF.Exp, scale=0.125),
                                  waits=((PE, sm),) +
                                        ((psb_war[kc % 3],) if psb_war[kc % 3] else ()),
                                  inc=AC)
                        exp_m[kc] = em
                        s_war[b] = (AC, em[1])
                    emit_pv(14)
                    emit_pv(15)
                    pv_pe = p.val(PE)
                    cv_m = [None, None]
                    for px in range(2):
                        hp = px * 64
                        cv1 = p.op("vector",
                                   lambda e, i=i, hp=hp, px=px, qt=qt: e.tensor_copy(
                                       ctxn[hp:hp + 64, i * 2048 + qt * 512:
                                            i * 2048 + (qt + 1) * 512],
                                       ps1[0:64, px * 512:(px + 1) * 512]),
                                   waits=((PE, pv_pe),), inc=DV)
                        cv_m[px] = cv1
                    cv2 = p.op("vector",
                               lambda e: e.tensor_copy(sums_row[0:1, 0:1024],
                                                       ps1[64:65, 0:1024]),
                               waits=((sums_war[0],) if sums_war[0] else ()),
                               inc=DV)
                    ctx_war[0] = ctx_war[1] = (DV, cv2[1])

                    def mk_norm(i=i, qt=qt, cv2=cv2, cv_m=cv_m):
                        def emit_norm():
                            p.op("scalar",
                                 lambda e: e.activation(sums_row[0:1, 0:1024],
                                                        sums_row[0:1, 0:1024], AF.Ln),
                                 waits=((DV, cv2[1]),), inc=AC)
                            rc1 = p.op("scalar",
                                       lambda e: e.activation(sums_row[0:1, 0:1024],
                                                              sums_row[0:1, 0:1024],
                                                              AF.Exp, scale=-1.0),
                                       inc=AC)
                            for px in range(2):
                                hp = px * 64
                                p.op("tensor",
                                     lambda e, px=px: e.matmul(
                                         ps1[0:64, 1024 + px * 512: 1024 + (px + 1) * 512],
                                         ones128_sb[0:1, 0:64],
                                         sums_row[0:1, px * 512:(px + 1) * 512],
                                         start=True, stop=True),
                                     waits=((AC, rc1[1]),) +
                                           ((mwar[px],) if mwar[px] else ()),
                                     inc=PE)
                                bc_pe = p.val(PE)
                                sums_war[0] = (PE, bc_pe)
                                st = p.op("vector",
                                          lambda e, i=i, hp=hp, px=px, qt=qt: e.tensor_tensor(
                                              ctxn[hp:hp + 64, i * 2048 + qt * 512:
                                                   i * 2048 + (qt + 1) * 512],
                                              ctxn[hp:hp + 64, i * 2048 + qt * 512:
                                                   i * 2048 + (qt + 1) * 512],
                                              ps1[0:64, 1024 + px * 512: 1024 + (px + 1) * 512],
                                              ALU.mult),
                                          waits=((PE, bc_pe), (DV, cv_m[px][1])), inc=DV)
                                mwar[px] = (DV, st[1])
                        return emit_norm

                    pending.append(mk_norm())
                if i + 2 < NH // 2:
                    assert not proj_q, "proj queue not drained"
                    ldm2 = emit_pair_loads(i + 2, buf)
                    proj_q.extend(mk_proj_groups(i + 2, buf, ldm2))
            for g in list(pending):
                g()
            pending.clear()
            for g, _ in list(proj_q):
                g()
            proj_q.clear()
            xnfree = wsl_war[0]
            kp_m = []
            for jc in range(8):
                sdma(f"kpl{jc}",
                     kp_sb[:, :].rearrange("p (kc t n) -> p kc t n", kc=8, t=8)
                     [:, :, jc, :],
                     KPS[:, :].rearrange("p (kc t n) -> p kc t n", kc=8, t=8)
                     [:, :, jc, :],
                     waits=(spill_done + (xnfree,)) if jc == 0 else ())
                kp_m.append((f"kpl{jc}", p.val(f"kpl{jc}")))
            p.barrier()
            sp_war = [None, None]
            pp_war = [None] * 16
            out_war = [None, None]
            xt_war = [None, None]
            osb_war = [None, None]
            rcpb2_war = None
            for qt in range(2):
                for Kc in range(8):
                    sdma(f"qpl{qt % 2}",
                         qp_sb[:, (qt % 2) * 8192 + Kc * 1024:
                               (qt % 2) * 8192 + (Kc + 1) * 1024],
                         QPS[:, Kc * 2048 + qt * 1024: Kc * 2048 + (qt + 1) * 1024],
                         waits=spill_done if (qt == 0 and Kc == 0) else ())
                ql_done = (f"qpl{qt % 2}", p.val(f"qpl{qt % 2}"))
                qoff = (qt % 2) * 8192
                exp2 = [None] * 16
                for ktc in range(16):
                    sb_ = ktc % 2
                    for Kc in range(8):
                        for ncc in range(2):
                            p.op("tensor",
                                 lambda e, ktc=ktc, Kc=Kc, ncc=ncc, sb_=sb_, qoff=qoff: e.matmul(
                                     ps0[:, sb_ * 1024 + ncc * 512: sb_ * 1024 + (ncc + 1) * 512],
                                     kp_sb[:, Kc * 2048 + ktc * 128: Kc * 2048 + ktc * 128 + 128],
                                     qp_sb[:, qoff + Kc * 1024 + ncc * 512:
                                           qoff + Kc * 1024 + (ncc + 1) * 512],
                                     start=(Kc == 0), stop=(Kc == 7)),
                                 waits=(((kp_m[ktc // 2],) if (qt == 0 and ktc % 2 == 0) else ()) +
                                        ((ql_done,) if ktc == 0 else ()) +
                                        ((sp_war[sb_],) if sp_war[sb_] else ())
                                        if (Kc == 0 and ncc == 0) else ()),
                                 inc=PE if (ncc == 1 and Kc == 7) else None)
                    sm = p.val(PE)
                    em = p.op("scalar",
                              lambda e, ktc=ktc, sb_=sb_: e.activation(
                                  pp_sb[:, ktc * 1024:(ktc + 1) * 1024],
                                  ps0[:, sb_ * 1024:(sb_ + 1) * 1024],
                                  AF.Exp, scale=1.0 / 32.0),
                              waits=((PE, sm),) + ((pp_war[ktc],) if pp_war[ktc] else ()),
                              inc=AC)
                    exp2[ktc] = em
                    sp_war[sb_] = (AC, em[1])
                for ktc in range(16):
                    for ncc in range(2):
                        p.op("tensor",
                             lambda e, ktc=ktc, ncc=ncc: e.matmul(
                                 ps1[0:1, ncc * 512:(ncc + 1) * 512], onescolb_sb[:, :],
                                 pp_sb[:, ktc * 1024 + ncc * 512: ktc * 1024 + (ncc + 1) * 512],
                                 start=(ktc == 0), stop=(ktc == 15)),
                             waits=(((AC, exp2[ktc][1]),) if ncc == 0 else ()) +
                                   ((rcpb2_war,) if (rcpb2_war and ktc == 0 and ncc == 0) else ()),
                             inc=PE if (ktc == 15 and ncc == 1) else None)
                sums_pe = p.val(PE)
                ln2 = p.op("scalar",
                           lambda e: e.activation(rcp2a[:, :], ps1[0:1, 0:1024], AF.Ln),
                           waits=((PE, sums_pe),), inc=AC)
                rc2 = p.op("scalar",
                           lambda e: e.activation(rcp2b[:, :], rcp2a[:, :],
                                                  AF.Exp, scale=-1.0),
                           inc=AC)
                for ncc in range(2):
                    p.op("tensor",
                         lambda e, ncc=ncc: e.matmul(
                             ps1[:, ncc * 512:(ncc + 1) * 512], ones128_sb[0:1, :],
                             rcp2b[0:1, ncc * 512:(ncc + 1) * 512],
                             start=True, stop=True),
                         waits=((AC, rc2[1]),), inc=PE if ncc == 1 else None)
                bc2_pe = p.val(PE)
                pn_ms = []
                for ktc in range(16):
                    pn = p.op("vector",
                              lambda e, ktc=ktc: e.tensor_tensor(
                                  pp_sb[:, ktc * 1024:(ktc + 1) * 1024],
                                  pp_sb[:, ktc * 1024:(ktc + 1) * 1024],
                                  ps1[:, 0:1024], ALU.mult),
                              waits=((PE, bc2_pe),) if ktc == 0 else (), inc=DV)
                    pn_ms.append(pn)
                pnorm_done = pn_ms[-1]
                rcpb2_war = (DV, pnorm_done[1])
                for Mc in range(8):
                    for Kc in range(8):
                        for ncc in range(2):
                            p.op("tensor",
                                 lambda e, Mc=Mc, Kc=Kc, ncc=ncc, qt=qt: e.matmul(
                                     ps1[:, 1024 + ncc * 512: 1024 + (ncc + 1) * 512],
                                     wstream[1][:, Kc * 1024 + Mc * 128: Kc * 1024 + Mc * 128 + 128],
                                     ctxn[:, Kc * 2048 + qt * 1024 + ncc * 512:
                                          Kc * 2048 + qt * 1024 + (ncc + 1) * 512],
                                     start=(Kc == 0), stop=False),
                                 waits=(((wsm_wo,) if (Mc == 0 and Kc == 0 and ncc == 0) else ()) +
                                        ((out_war[ncc],) if (out_war[ncc] and Kc == 0) else ())))
                    for ktc in range(16):
                        for ncc in range(2):
                            p.op("tensor",
                                 lambda e, Mc=Mc, ktc=ktc, ncc=ncc: e.matmul(
                                     ps1[:, 1024 + ncc * 512: 1024 + (ncc + 1) * 512],
                                     vp_bf[:, ktc * 1024 + Mc * 128: ktc * 1024 + Mc * 128 + 128],
                                     pp_sb[:, ktc * 1024 + ncc * 512: ktc * 1024 + (ncc + 1) * 512],
                                     start=False, stop=(ktc == 15)),
                                 waits=((DV, pn_ms[ktc][1]),) if (Mc == 0 and ncc == 0) else (),
                                 inc=PE if ktc == 15 else None)
                    pm_last = p.val(PE)
                    for ncc in range(2):
                        uoff = 1024 + ncc * 512
                        pm = pm_last - 1 + ncc
                        xb = ncc % 2
                        xt = sdma(f"xtl{xb}", x_tile[xb][:, :],
                                  X[seq, Mc * 128:(Mc + 1) * 128,
                                    qt * 1024 + ncc * 512: qt * 1024 + (ncc + 1) * 512],
                                  waits=(xt_war[xb],) if xt_war[xb] else ())
                        ob = ncc % 2
                        ev = p.op("vector",
                                  lambda e, Mc=Mc, uoff=uoff, xb=xb, ob=ob: e.scalar_tensor_tensor(
                                      out_sb[ob][:, :], ps1[:, uoff:uoff + 512],
                                      bcols_sb[:, 16 + Mc:17 + Mc], x_tile[xb][:, :],
                                      ALU.add, ALU.add),
                                  waits=((PE, pm), xt) +
                                        ((osb_war[ob],) if osb_war[ob] else ()),
                                  inc=DV)
                        out_war[ncc] = (DV, ev[1])
                        xt_war[xb] = (DV, ev[1])
                        dm = _dma("gpsimd" if ob == 0 else "scalar",
                                  f"outs{ob}",
                                  OUT[seq, Mc * 128:(Mc + 1) * 128,
                                      qt * 1024 + ncc * 512: qt * 1024 + (ncc + 1) * 512],
                                  out_sb[ob][:, :], waits=(ev,))
                        osb_war[ob] = dm
                mark = (PE, p.val(PE))
                for ktc in range(16):
                    pp_war[ktc] = mark
            p.barrier()

        p.op("gpsimd", None, waits=(("outs0", p.val("outs0")),
                                    ("outs1", p.val("outs1"))))
        assert len(p.order) <= 200, f"too many semaphores: {len(p.order)}"
        p.emit(nc, ctx)
    return nc


def _host_inputs(inputs):
    f32 = np.float32
    asf = lambda a: np.ascontiguousarray(np.asarray(a), dtype=f32)
    d = {}
    d["wq"] = asf(np.asarray(inputs["L0_attn_Wq"]).T)
    d["wk"] = asf(np.asarray(inputs["L0_attn_Wk"]).T)
    d["wv"] = asf(np.asarray(inputs["L0_attn_Wv"]).T)
    d["wo"] = asf(np.asarray(inputs["L0_attn_Wo"]).T)
    for f in "qkv":
        w1 = np.asarray(inputs[f"L0_ffn_{f}_w1"])
        w2 = np.asarray(inputs[f"L0_ffn_{f}_w2"])
        d["w1" + f] = asf(w1.transpose(1, 0, 2).reshape(H, H))
        d["w2" + f] = asf(w2.reshape(H, H))
        d["b2" + f] = asf(inputs[f"L0_ffn_{f}_b2"])
    nw = np.concatenate([np.asarray(inputs[f"L0_ffn_{f}_node_w"]) for f in "qkv"], axis=0)
    nwt = np.zeros((H, 96), f32)
    for fi in range(3):
        nwt[:, fi * 32: fi * 32 + 15] = nw[fi * 15:(fi + 1) * 15].T
    d["nwt"] = nwt
    nodebc = np.zeros((96, 1), f32)
    for fi, f in enumerate("qkv"):
        nb = np.asarray(inputs[f"L0_ffn_{f}_node_b"], f32)
        nodebc[fi * 32: fi * 32 + 15, 0] = nb
        nodebc[fi * 32 + 15, 0] = 1.0
    d["nodebc"] = nodebc
    bcols = np.zeros((128, 48), f32)
    bcols[:, 0:8] = np.asarray(inputs["L0_attn_bq"], f32).reshape(8, 128).T
    bcols[:, 8:16] = np.asarray(inputs["L0_attn_bk"], f32).reshape(8, 128).T
    bcols[:, 16:24] = np.asarray(inputs["L0_attn_bo"], f32).reshape(8, 128).T
    for i, f in enumerate("qkv"):
        b1 = np.asarray(inputs[f"L0_ffn_{f}_b1"], f32).reshape(-1)
        bcols[:, 24 + 8 * i: 32 + 8 * i] = b1.reshape(8, 128).T
    d["bcols"] = bcols
    d["bvr"] = asf(inputs["L0_attn_bv"]).reshape(1, H)
    return d


def kernel(**inputs):
    x = np.asarray(inputs["L0_attn_hidden_states"], np.float32)
    xt = np.ascontiguousarray(x.transpose(0, 2, 1))
    wd = _host_inputs(inputs)
    nc = build(NSEQ)
    in_maps = []
    for c in range(NCORES):
        m = {"x": np.ascontiguousarray(xt[c * NSEQ:(c + 1) * NSEQ])}
        m.update(wd)
        in_maps.append(m)
    res = run_bass_kernel_spmd(nc, in_maps, core_ids=list(range(NCORES)))
    out = np.concatenate([r["out"] for r in res.results], axis=0)
    return np.ascontiguousarray(out.transpose(0, 2, 1))


# version 24 (final)
# speedup vs baseline: 1.0310x; 1.0310x over previous
import numpy as np
import ml_dtypes
from contextlib import ExitStack

import concourse.bass as bass
import concourse.mybir as mybir
from concourse.bass_utils import run_bass_kernel_spmd

F32 = mybir.dt.float32
BF16 = mybir.dt.bfloat16
AF = mybir.ActivationFunctionType
ALU = mybir.AluOpType

NCORES = 8
NSEQ = 4
S = 2048
H = 1024
NH, HD = 16, 64
NL, LW = 16, 64
BIG = 32768.0

ENGINES = ("sync", "vector", "tensor", "scalar", "gpsimd")


class Prog:

    def __init__(self):
        self.items = {e: [] for e in ENGINES}
        self.counts = {}
        self.order = []

    def _sem(self, name):
        if name not in self.counts:
            self.counts[name] = 0
            self.order.append(name)

    def val(self, name):
        self._sem(name)
        return self.counts[name]

    def op(self, eng, fn, waits=(), inc=None, n=1):
        waits = tuple((s, v) for (s, v) in waits if v > 0)
        for s, v in waits:
            self._sem(s)
        if inc is not None:
            self._sem(inc)
            self.counts[inc] += n
        self.items[eng].append((waits, fn, inc, n))
        return (inc, self.counts[inc]) if inc is not None else None

    def barrier(self):
        self._sem("stg")
        target = self.counts["stg"] + len(ENGINES)
        for e in ENGINES:
            self.op(e, lambda eng: eng.nop(), inc="stg")
        for e in ENGINES:
            self.op(e, None, waits=(("stg", target),))
        return target

    def emit(self, nc, ctx):
        handles = {name: ctx.enter_context(nc.semaphore(name)) for name in self.order}
        blk = ctx.enter_context(nc.Block())

        def runner(engname):
            def body(eng):
                for waits, fn, inc, n in self.items[engname]:
                    for s, v in waits:
                        eng.wait_ge(handles[s], v)
                    if fn is None:
                        continue
                    ins = fn(eng)
                    if inc is not None:
                        ins.then_inc(handles[inc], n)
            return body

        blk.sync(runner("sync"))
        blk.vector(runner("vector"))
        blk.tensor(runner("tensor"))
        blk.scalar(runner("scalar"))
        blk.gpsimd(runner("gpsimd"))


def _leaf_matrices():
    E = np.zeros((16, 16), np.float32)
    for leaf in range(16):
        b = [(leaf >> 3) & 1, (leaf >> 2) & 1, (leaf >> 1) & 1, leaf & 1]
        nodes = [0, 1 + b[0], 3 + 2 * b[0] + b[1], 7 + 4 * b[0] + 2 * b[1] + b[2]]
        z = sum(1 for x in b if x == 0)
        for j, n in enumerate(nodes):
            E[n, leaf] += 2 * b[j] - 1.0
        E[15, leaf] += z
    D = BIG * E.copy()
    D[15, :] = BIG * (E[15, :] - 4.0)
    return E, D


def build(n_seq=NSEQ, dbg=False):
    nc = bass.Bass("TRN2", target_bir_lowering=False, debug=False)

    X = nc.dram_tensor("x", [n_seq, H, S], F32, kind="ExternalInput").ap()
    OUT = nc.dram_tensor("out", [n_seq, H, S], F32, kind="ExternalOutput").ap()
    win = {}
    for nm in ("wq", "wk", "wv", "wo", "w1q", "w2q", "w1k", "w2k", "w1v", "w2v"):
        win[nm] = nc.dram_tensor(nm, [H, H], F32, kind="ExternalInput").ap()
    B2 = {f: nc.dram_tensor("b2" + f, [16, H], F32, kind="ExternalInput").ap()
          for f in "qkv"}
    NWT = nc.dram_tensor("nwt", [H, 96], F32, kind="ExternalInput").ap()
    NODEBC = nc.dram_tensor("nodebc", [96, 1], F32, kind="ExternalInput").ap()
    BCOLS = nc.dram_tensor("bcols", [128, 48], F32, kind="ExternalInput").ap()
    BVR = nc.dram_tensor("bvr", [1, H], F32, kind="ExternalInput").ap()

    WS = nc.dram_tensor("ws", [10, 128, 8192], BF16, kind="Internal").ap()
    spk = "ExternalOutput" if dbg else "Internal"
    QPS = nc.dram_tensor("qps", [128, 16384], BF16, kind=spk).ap()
    KPS = nc.dram_tensor("kps", [128, 16384], BF16, kind=spk).ap()
    DBG = {}
    if dbg:
        for nm, shape, dt in (("d_xn", [128, 16384], BF16), ("d_g", [96, 2048], BF16),
                              ("d_sel", [80, 2048], BF16), ("d_q", [128, 16384], BF16),
                              ("d_k", [128, 16384], BF16), ("d_v", [128, 16640], BF16),
                              ("d_vp", [128, 16384], BF16), ("d_ctxn", [128, 16384], BF16),
                              ("d_murow", [1, 2048], F32), ("d_var", [1, 2048], F32),
                              ("d_rstd", [1, 2048], F32), ("d_mub", [128, 2048], F32),
                              ("d_rstdb", [128, 2048], F32)):
            DBG[nm] = nc.dram_tensor(nm, shape, dt, kind="ExternalOutput").ap()
    WIDX = {"wq": 0, "wk": 1, "wv": 2, "wo": 3, "w1q": 4, "w2q": 5,
            "w1k": 6, "w2k": 7, "w1v": 8, "w2v": 9}

    E, D = _leaf_matrices()
    cbig_np = np.repeat(D, LW, axis=1).astype(ml_dtypes.bfloat16)
    csmall_np = E.astype(ml_dtypes.bfloat16)
    CBIG = nc.inline_tensor(cbig_np, name="cbig").ap()
    CSMALL = nc.inline_tensor(csmall_np, name="csmall").ap()
    ONES128 = nc.inline_tensor(np.ones((1, 128), np.float32), name="ones128").ap()
    ONESCOLF = nc.inline_tensor(np.ones((128, 1), np.float32), name="onescolf").ap()
    ONESCOLB = nc.inline_tensor(np.ones((128, 1), ml_dtypes.bfloat16), name="onescolb").ap()

    with ExitStack() as ctx:
        at = lambda name, shape, dt, off: nc.alloc_sbuf_tensor_at(name, shape, dt, offset=off)

        A, B, C, Dr, Er, Fr = 16512, 82048, 115328, 148096, 180864, 214656
        xst = [at("xst0", [128, 2048], F32, A),
               at("xst1", [128, 2048], F32, A + 8192)]
        x2b = [at("x2b0", [128, 2048], F32, A + 16384),
               at("x2b1", [128, 2048], F32, A + 24576)]
        w1sb = [at("w1sb0", [128, 8192], BF16, A),
                at("w1sb1", [128, 8192], BF16, A + 16384)]
        w2sb = [at("w2sb0", [128, 8192], BF16, A + 32768),
                at("w2sb1", [128, 8192], BF16, A + 49152)]
        qk_t = [[at("q_t0", [128, 2048], BF16, A),
                 at("q_t1", [128, 2048], BF16, A + 4096)],
                [at("k_t0", [128, 2048], BF16, A + 8192),
                 at("k_t1", [128, 2048], BF16, A + 12288)]]
        ctxn = at("ctxn", [128, 16384], BF16, A + 32768)
        qp_sb = at("qp_sb", [128, 16384], BF16, A)
        kp_sb = at("kp_sb", [128, 16384], BF16, C)
        wst_f = [at("wstf0", [128, 1024], F32, Dr + 16384),
                 at("wstf1", [128, 1024], F32, Dr + 20480)]
        wst_b = [at("wstb0", [128, 1024], BF16, Dr + 24576),
                 at("wstb1", [128, 1024], BF16, Dr + 26624)]
        b2tmp = at("b2tmp", [16, 1024], F32, Er + 12288)
        bvtmp = at("bvtmp", [1, 1024], F32, Er + 16384)
        hid = [at("hid0", [128, 4096], BF16, B),
               at("hid1", [128, 4096], BF16, B + 8192)]
        spst = [at(f"spst{i}", [128, 512], BF16, B + 16384 + i * 1024) for i in range(4)]
        g_sb = at("g_sb", [96, 2048], BF16, B + 20480)
        selreg = at("selreg", [80, 2048], BF16, B + 24576)
        v_bf = at("v_bf", [128, 16640], BF16, B)
        pp_sb = at("pp_sb", [128, 16384], BF16, B)
        xn_bf = at("xn_bf", [128, 16384], BF16, C)
        mu_b = at("mu_b", [128, 2048], F32, Dr)
        rstd_b = at("rstd_b", [128, 2048], F32, Dr + 8192)
        vp_bf = at("vp_bf", [128, 16384], BF16, Dr)
        mu_row = at("mu_row", [1, 2048], F32, Er)
        var_row = at("var_row", [1, 2048], F32, Er + 8192)
        rstd_row = at("rstd_row", [1, 2048], F32, Er + 16384)
        wqs = at("wqs", [128, 1024], BF16, Er)
        wks = at("wks", [128, 1024], BF16, Er + 2048)
        wvs = at("wvs", [128, 1024], BF16, Er + 4096)
        p_sb = [at(f"psb{i}", [128, 1024], BF16, Er + 6144 + 2048 * i) for i in range(3)]
        sums_row = at("sums_row", [1, 1024], F32, Er + 12288)
        wstream = [None,
                   at("wstrB", [128, 8192], BF16, Er + 16384)]
        out_sb = [at("outsb0", [128, 512], F32, Er),
                  at("outsb1", [128, 512], F32, Er + 2048)]
        x_tile = [at("xt0", [128, 512], F32, Er + 4096),
                  at("xt1", [128, 512], F32, Er + 6144)]
        rcp2a = at("rcp2a", [1, 1024], F32, Er + 8192)
        rcp2b = at("rcp2b", [1, 1024], F32, Er + 12288)
        bcols_sb = at("bcols_sb", [128, 48], F32, Fr)
        nodebc_sb = at("nodebc_sb", [96, 1], F32, Fr + 192)
        ones128_sb = at("ones128_sb", [1, 128], F32, Fr + 224)
        onescolf_sb = at("onescolf_sb", [128, 1], F32, Fr + 736)
        onescolb_sb = at("onescolb_sb", [128, 1], BF16, Fr + 768)
        cbig_sb = at("cbig_sb", [80, 1024], BF16, Fr + 800)
        csmall_sb = at("csmall_sb", [80, 16], BF16, Fr + 2848)
        nwt_sb = at("nwt_sb", [128, 768], F32, Fr + 2880)
        b2rep = at("b2rep", [80, 1024], BF16, Fr + 5952)
        bv_bcast = at("bv_bcast", [128, 1024], F32, Fr + 8000)

        ps0 = ctx.enter_context(nc.psum_tensor("ps0", [128, 2048], F32))
        ps1 = ctx.enter_context(nc.psum_tensor("ps1", [128, 2048], F32))

        p = Prog()
        PE, DV, AC = "pe", "dv", "ac"
        dma_sems = set()

        def _dma(eng, sem_name, out, in_, waits=()):
            dma_sems.add(sem_name)
            m = p.op(eng, lambda e, o=out, i=in_: e.dma_start(out=o, in_=i),
                     waits=waits, inc=sem_name, n=16)
            return m

        def sdma(sem_name, out, in_, waits=()):
            return _dma("sync", sem_name, out, in_, waits)

        def pdma(sem_name, out, in_, waits=()):
            return _dma("gpsimd", sem_name, out, in_, waits)

        def tap(name, src):
            if not dbg:
                return
            pdma("tapsem", DBG[name], src)
            t = p.val("tapsem")
            for e in ENGINES:
                p.op(e, None, waits=(("tapsem", t),))

        _orig_op = p.op

        def _checked_op(eng, fn, waits=(), inc=None, n=1):
            for s, v in waits:
                if s in dma_sems and v > 0:
                    assert v == p.counts.get(s, 0), \
                        f"unsound DMA wait: {s} at {v} != full count {p.counts.get(s, 0)}"
            return _orig_op(eng, fn, waits=waits, inc=inc, n=n)

        p.op = _checked_op

        sdma("cst", nwt_sb[:, 0:768].rearrange("p (c n) -> p c n", c=8),
             NWT.rearrange("(c p) n -> p c n", p=128))
        sdma("cst", bcols_sb[:, :], BCOLS)
        sdma("cst", nodebc_sb[:, :], NODEBC)
        sdma("cst", ones128_sb[:, :], ONES128)
        sdma("cst", onescolf_sb[:, :], ONESCOLF)
        sdma("cst", onescolb_sb[:, :], ONESCOLB)
        for fi in range(3):
            sdma("cst", cbig_sb[fi * 32:fi * 32 + 16, :], CBIG)
            sdma("cst", csmall_sb[fi * 32:fi * 32 + 16, :], CSMALL)
        sdma("cst", bvtmp[:, :], BVR)
        consts_m = ("cst", p.val("cst"))

        wst_done = {}
        wfs_done = None
        prev_b2 = None
        for fi, f in enumerate("qkv"):
            lm = sdma("b2l", b2tmp[:, :], B2[f], waits=(prev_b2,) if prev_b2 else ())
            cm = p.op("vector",
                      lambda e, fi=fi: e.tensor_copy(b2rep[fi * 32:fi * 32 + 16, :],
                                                     b2tmp[:, :]),
                      waits=(lm,), inc=DV)
            prev_b2 = (DV, cm[1])
        for ncc in range(2):
            p.op("tensor",
                 lambda e, ncc=ncc: e.matmul(ps0[:, ncc * 512:(ncc + 1) * 512],
                                             ones128_sb[0:1, :],
                                             bvtmp[0:1, ncc * 512:(ncc + 1) * 512],
                                             start=True, stop=True),
                 waits=((consts_m,) if ncc == 0 else ()), inc=PE)
        bv_pe = p.val(PE)
        p.op("scalar", lambda e: e.copy(bv_bcast[:, :], ps0[:, 0:1024]),
             waits=((PE, bv_pe),), inc=AC)
        p.op("sync", None, waits=(consts_m, ("b2l", p.val("b2l"))))
        p.barrier()

        qk_evac_bias = {"q": bcols_sb[:, 0:8], "k": bcols_sb[:, 8:16]}
        b1c = {"q": bcols_sb[:, 24:32], "k": bcols_sb[:, 32:40], "v": bcols_sb[:, 40:48]}

        for seq in range(n_seq):
            xw = [None, None]
            xwd = [None, None]
            x2w = [None, None]
            for c in range(8):
                j = c % 2
                lm = sdma(f"xl{j}", xst[j][:, :], X[seq, c * 128:(c + 1) * 128, :],
                          waits=((xw[j],) if xw[j] else ()) +
                                ((xwd[j],) if xwd[j] else ()))
                sq = p.op("vector",
                          lambda e, j=j: e.tensor_mul(x2b[j][:, :], xst[j][:, :],
                                                      xst[j][:, :]),
                          waits=(lm,) + ((x2w[j],) if x2w[j] else ()), inc=DV)
                xwd[j] = (DV, sq[1])
                for ncc in range(4):
                    p.op("tensor",
                         lambda e, j=j, ncc=ncc, c=c: e.matmul(
                             ps0[0:1, ncc * 512:(ncc + 1) * 512], onescolf_sb[:, :],
                             xst[j][:, ncc * 512:(ncc + 1) * 512],
                             start=(c == 0), stop=(c == 7)),
                         waits=((lm,) if ncc == 0 else ()),
                         inc=PE if ncc == 3 else None)
                xw[j] = (PE, p.val(PE))
                for ncc in range(4):
                    p.op("tensor",
                         lambda e, j=j, ncc=ncc, c=c: e.matmul(
                             ps1[0:1, ncc * 512:(ncc + 1) * 512], onescolf_sb[:, :],
                             x2b[j][:, ncc * 512:(ncc + 1) * 512],
                             start=(c == 0), stop=(c == 7)),
                         waits=(((DV, sq[1]),) if ncc == 0 else ()),
                         inc=PE if ncc == 3 else None)
                x2w[j] = (PE, p.val(PE))
            p1_pe = p.val(PE)
            mu_m = p.op("vector",
                        lambda e: e.tensor_scalar(mu_row[:, :], ps0[0:1, 0:2048],
                                                  1.0 / H, None, ALU.mult),
                        waits=((PE, p1_pe),), inc=DV)
            for ncc in range(4):
                p.op("tensor",
                     lambda e, ncc=ncc: e.matmul(ps0[:, ncc * 512:(ncc + 1) * 512],
                                                 ones128_sb[0:1, :],
                                                 mu_row[0:1, ncc * 512:(ncc + 1) * 512],
                                                 start=True, stop=True),
                     waits=(((DV, mu_m[1]),) if ncc == 0 else ()),
                     inc=PE if ncc == 3 else None)
            mub_pe = p.val(PE)
            mub_m = p.op("scalar", lambda e: e.copy(mu_b[:, :], ps0[:, 0:2048]),
                         waits=((PE, mub_pe),), inc=AC)
            mu2_m = p.op("vector",
                         lambda e: e.tensor_mul(rstd_row[:, :], mu_row[:, :], mu_row[:, :]),
                         waits=((PE, mub_pe),), inc=DV)
            var_m = p.op("vector",
                         lambda e: e.scalar_tensor_tensor(var_row[:, :], ps1[0:1, 0:2048],
                                                          1.0 / H, rstd_row[:, :],
                                                          ALU.mult, ALU.subtract),
                         inc=DV)
            vare_m = p.op("vector",
                          lambda e: e.tensor_scalar(var_row[:, :], var_row[:, :],
                                                    1e-5, None, ALU.add),
                          inc=DV)
            ln_m = p.op("scalar",
                        lambda e: e.activation(mu_row[:, :], var_row[:, :], AF.Ln),
                        waits=((DV, vare_m[1]),), inc=AC)
            rst_m = p.op("scalar",
                         lambda e: e.activation(rstd_row[:, :], mu_row[:, :], AF.Exp,
                                                scale=-0.5),
                         waits=((DV, mu2_m[1]),), inc=AC)
            for ncc in range(4):
                p.op("tensor",
                     lambda e, ncc=ncc: e.matmul(ps0[:, ncc * 512:(ncc + 1) * 512],
                                                 ones128_sb[0:1, :],
                                                 rstd_row[0:1, ncc * 512:(ncc + 1) * 512],
                                                 start=True, stop=True),
                     waits=(((AC, rst_m[1]), (AC, mub_m[1])) if ncc == 0 else ()),
                     inc=PE if ncc == 3 else None)
            rb_pe = p.val(PE)
            rb_m = p.op("scalar", lambda e: e.copy(rstd_b[:, :], ps0[:, 0:2048]),
                        waits=((PE, rb_pe),), inc=AC)
            if seq > 0:
                for wi, (nm, dst) in enumerate((("wq", wqs), ("wk", wks), ("wv", wvs))):
                    sdma(f"w{nm[1]}s",
                         dst[:, :].rearrange("p (kc n) -> p kc n", kc=8),
                         WS[WIDX[nm], :, :].rearrange("p (kc m n) -> p kc m n", kc=8, m=8)
                         [:, :, 0, :],
                         waits=((AC, rst_m[1]),) if wi == 0 else ())
                ldm_pair0 = (("wqs", p.val("wqs")), ("wks", p.val("wks")),
                             ("wvs", p.val("wvs")))
            bufs4 = [xst[0], xst[1], x2b[0], x2b[1]]
            pxw = [xw[0], xw[1], x2w[0], x2w[1]]
            pxd = [xwd[0], xwd[1], xwd[0], xwd[1]]
            for c in range(8):
                b = c % 4
                lm = sdma(f"xl{b}", bufs4[b][:, :], X[seq, c * 128:(c + 1) * 128, :],
                          waits=((pxw[b],) if pxw[b] else ()) +
                                ((pxd[b],) if pxd[b] else ()))
                s1 = p.op("vector",
                          lambda e, b=b: e.tensor_sub(bufs4[b][:, :], bufs4[b][:, :],
                                                      mu_b[:, :]),
                          waits=(lm, (AC, rb_m[1])), inc=DV)
                s2 = p.op("vector",
                          lambda e, b=b: e.tensor_mul(bufs4[b][:, :], bufs4[b][:, :],
                                                      rstd_b[:, :]),
                          inc=DV)
                ct = p.op("scalar",
                          lambda e, b=b, c=c: e.copy(
                              xn_bf[:, c * 2048:(c + 1) * 2048], bufs4[b][:, :]),
                          waits=((DV, s2[1]),), inc=AC)
                pxd[b] = (AC, ct[1])
                for ncc in range(4):
                    p.op("tensor",
                         lambda e, b=b, ncc=ncc, c=c: e.matmul(
                             ps1[0:96, ncc * 512:(ncc + 1) * 512],
                             nwt_sb[:, c * 96:(c + 1) * 96],
                             bufs4[b][:, ncc * 512:(ncc + 1) * 512],
                             start=(c == 0), stop=(c == 7)),
                         waits=(((DV, s2[1]),) if ncc == 0 else ()),
                         inc=PE if ncc == 3 else None)
                pxw[b] = (PE, p.val(PE))
            node_pe = p.val(PE)
            g_m = p.op("vector",
                       lambda e: e.tensor_scalar(g_sb[:, :], ps1[0:96, 0:2048],
                                                 nodebc_sb[:, 0:1], 0.0,
                                                 ALU.add, ALU.is_ge),
                       waits=((PE, node_pe),), inc=DV)
            sel_dv = None
            for fi in range(3):
                for ncc in range(4):
                    p.op("tensor",
                         lambda e, fi=fi, ncc=ncc: e.matmul(
                             ps1[0:16, ncc * 512:(ncc + 1) * 512],
                             csmall_sb[fi * 32:fi * 32 + 16, :],
                             g_sb[fi * 32:fi * 32 + 16, ncc * 512:(ncc + 1) * 512],
                             start=True, stop=True),
                         waits=((DV, g_m[1]),) + (((DV, sel_dv[1]),) if sel_dv else ()),
                         inc=PE if ncc == 3 else None)
                sp = p.val(PE)
                sel_dv = p.op("vector",
                              lambda e, fi=fi: e.tensor_scalar(
                                  selreg[fi * 32:fi * 32 + 16, :], ps1[0:16, 0:2048],
                                  3.5, None, ALU.is_ge),
                              waits=((PE, sp),), inc=DV)
            p.barrier()
            if seq == 0:
                tap("d_xn", xn_bf[:, :])
                tap("d_g", g_sb[:, :])
                tap("d_sel", selreg[:, :])
                tap("d_murow", mu_row[:, :])
                tap("d_var", var_row[:, :])
                tap("d_rstd", rstd_row[:, :])
                tap("d_mub", mu_b[:, :])
                tap("d_rstdb", rstd_b[:, :])

            if seq == 0:
                war = {}
                for oi, nm in enumerate(("w1q", "w2q", "w1k", "w2k", "w1v",
                                         "w2v", "wq", "wk", "wv", "wo")):
                    for c in range(8):
                        j = (oi * 8 + c) % 2
                        lm = _dma("scalar", f"wfl{j}", wst_f[j][:, :],
                                  win[nm][c * 128:(c + 1) * 128, :])
                        cm = p.op("scalar",
                                  lambda e, j=j: e.copy(wst_b[j][:, :],
                                                        wst_f[j][:, :]),
                                  waits=(lm,) + ((war[("b", j)],) if ("b", j) in war else ()),
                                  inc=AC)
                        sm = _dma("scalar", f"wf{j}_{nm}",
                                  WS[WIDX[nm], :, c * 1024:(c + 1) * 1024],
                                  wst_b[j][:, :])
                        war[("b", j)] = sm
                    wst_done[nm] = ((f"wf0_{nm}", p.val(f"wf0_{nm}")),
                                    (f"wf1_{nm}", p.val(f"wf1_{nm}")))
                wfs_done = wst_done["wo"]

            wl_m = {}

            def emit_fff_loads(fi, waits=()):
                f = "qkv"[fi]
                j = fi % 2
                if seq == 0:
                    waits = tuple(waits) + wst_done["w2" + f]
                for c in range(8):
                    sdma(f"w1l{j}", w1sb[j][:, c * 1024:(c + 1) * 1024],
                         WS[WIDX["w1" + f], :, c * 1024:(c + 1) * 1024],
                         waits=waits if c == 0 else ())
                    sdma(f"w2l{j}", w2sb[j][:, c * 1024:(c + 1) * 1024],
                         WS[WIDX["w2" + f], :, c * 1024:(c + 1) * 1024])
                wl_m[("ld", fi)] = ((f"w1l{j}", p.val(f"w1l{j}")),
                                    (f"w2l{j}", p.val(f"w2l{j}")))

            emit_fff_loads(0)
            emit_fff_loads(1)
            hid_war = [None, None]
            psu_war = [None] * 4
            psw_war = [None] * 4
            spst_war = [None] * 4
            for fi, f in enumerate("qkv"):
                j = fi % 2
                lm = wl_m[("ld", fi)]
                for tc in range(4):
                    hb = tc % 2
                    ev_ms = []
                    for Mc in range(8):
                        u = (tc * 8 + Mc) % 4
                        for Kc in range(8):
                            p.op("tensor",
                                 lambda e, j=j, Mc=Mc, Kc=Kc, tc=tc, u=u: e.matmul(
                                     ps0[:, u * 512:(u + 1) * 512],
                                     w1sb[j][:, Kc * 1024 + Mc * 128: Kc * 1024 + Mc * 128 + 128],
                                     xn_bf[:, Kc * 2048 + tc * 512: Kc * 2048 + (tc + 1) * 512],
                                     start=(Kc == 0), stop=False),
                                 waits=(lm if (tc == 0 and Mc == 0 and Kc == 0) else ()) +
                                       ((psu_war[u],) if (psu_war[u] and Kc == 0) else ()))
                        pm = p.op("tensor",
                                  lambda e, fi=fi, Mc=Mc, tc=tc, u=u: e.matmul(
                                      ps0[:, u * 512:(u + 1) * 512],
                                      cbig_sb[fi * 32:fi * 32 + 16, Mc * 128:(Mc + 1) * 128],
                                      g_sb[fi * 32:fi * 32 + 16, tc * 512:(tc + 1) * 512],
                                      start=False, stop=True),
                                  inc=PE)
                        ev = p.op("vector",
                                  lambda e, f=f, Mc=Mc, hb=hb, u=u: e.tensor_scalar(
                                      hid[hb][:, Mc * 512:(Mc + 1) * 512],
                                      ps0[:, u * 512:(u + 1) * 512],
                                      b1c[f][:, Mc:Mc + 1], 0.0, ALU.add, ALU.max),
                                  waits=((PE, pm[1]),) +
                                        ((hid_war[hb],) if (hid_war[hb] and Mc == 0) else ()),
                                  inc=DV)
                        psu_war[u] = (DV, ev[1])
                        ev_ms.append(ev)
                    hid_done = ev_ms[-1]
                    if f in ("q", "k"):
                        for Oc in range(8):
                            u = (tc * 8 + Oc) % 4
                            for Kc in range(8):
                                p.op("tensor",
                                     lambda e, j=j, Oc=Oc, Kc=Kc, hb=hb, u=u: e.matmul(
                                         ps1[:, u * 512:(u + 1) * 512],
                                         w2sb[j][:, Kc * 1024 + Oc * 128: Kc * 1024 + Oc * 128 + 128],
                                         hid[hb][:, Kc * 512:(Kc + 1) * 512],
                                         start=(Kc == 0), stop=False),
                                     waits=((DV, ev_ms[Kc][1]),) +
                                           ((psw_war[u],) if (psw_war[u] and Kc == 0) else ()))
                            sm = p.op("tensor",
                                      lambda e, fi=fi, Oc=Oc, tc=tc, u=u: e.matmul(
                                          ps1[:, u * 512:(u + 1) * 512],
                                          b2rep[fi * 32:fi * 32 + 16, Oc * 128:(Oc + 1) * 128],
                                          selreg[fi * 32:fi * 32 + 16, tc * 512:(tc + 1) * 512],
                                          start=False, stop=True),
                                      inc=PE)
                            sbuf_i = (tc * 8 + Oc) % 4
                            cv = p.op("vector",
                                      lambda e, sbuf_i=sbuf_i, u=u: e.tensor_copy(
                                          spst[sbuf_i][:, :], ps1[:, u * 512:(u + 1) * 512]),
                                      waits=((PE, sm[1]),) +
                                            ((spst_war[sbuf_i],) if spst_war[sbuf_i] else ()),
                                      inc=DV)
                            psw_war[u] = (DV, cv[1])
                            dst = (QPS if f == "q" else KPS)
                            dm = pdma(f"sps{sbuf_i}",
                                      dst[:, Oc * 2048 + tc * 512: Oc * 2048 + (tc + 1) * 512],
                                      spst[sbuf_i][:, :], waits=(cv,))
                            spst_war[sbuf_i] = dm
                    else:
                        for ts4 in range(4):
                            t128 = tc * 512 + ts4 * 128
                            for ncc in range(2):
                                u = (tc * 8 + ts4 * 2 + ncc) % 4
                                for Kc in range(8):
                                    p.op("tensor",
                                         lambda e, j=j, Kc=Kc, hb=hb, ts4=ts4, ncc=ncc, u=u: e.matmul(
                                             ps1[:, u * 512:(u + 1) * 512],
                                             hid[hb][:, Kc * 512 + ts4 * 128: Kc * 512 + ts4 * 128 + 128],
                                             w2sb[j][:, Kc * 1024 + ncc * 512: Kc * 1024 + (ncc + 1) * 512],
                                             start=(Kc == 0), stop=False),
                                         waits=((DV, hid_done[1]),) +
                                               ((psw_war[u],) if (psw_war[u] and Kc == 0) else ()))
                                sm = p.op("tensor",
                                          lambda e, t128=t128, ncc=ncc, u=u: e.matmul(
                                              ps1[:, u * 512:(u + 1) * 512],
                                              selreg[64:80, t128:t128 + 128],
                                              b2rep[64:80, ncc * 512:(ncc + 1) * 512],
                                              start=False, stop=True),
                                          inc=PE)
                                cv = p.op("vector",
                                          lambda e, tc=tc, ts4=ts4, ncc=ncc, u=u: e.tensor_copy(
                                              vp_bf[:, (tc * 4 + ts4) * 1024 + ncc * 512:
                                                    (tc * 4 + ts4) * 1024 + (ncc + 1) * 512],
                                              ps1[:, u * 512:(u + 1) * 512]),
                                          waits=((PE, sm[1]),) +
                                                (wfs_done if (seq == 0 and tc == 0 and
                                                              ts4 == 0 and ncc == 0) else ()),
                                          inc=DV)
                                psw_war[u] = (DV, cv[1])
                    hid_war[hb] = (PE, p.val(PE))
                wl_m[("done", fi)] = (PE, p.val(PE))
                if fi + 2 < 3:
                    emit_fff_loads(fi + 2, waits=(wl_m[("done", fi)],))
            spill_done = tuple((f"sps{i}", p.val(f"sps{i}")) for i in range(4))
            p.barrier()
            if seq == 0:
                tap("d_vp", vp_bf[:, :])

            vones = p.op("vector",
                         lambda e: e.memset(
                             v_bf[:, :].rearrange("p (c h w) -> p c h w", c=16, h=16)
                             [:, :, :, 64:65], 1.0), inc=DV)
            for c in range(8):
                sdma("wsl1", wstream[1][:, c * 1024:(c + 1) * 1024],
                     WS[WIDX["wo"], :, c * 1024:(c + 1) * 1024],
                     waits=wfs_done if (seq == 0 and c == 0) else ())
            wsm_wo = ("wsl1", p.val("wsl1"))
            proj_pe_last = [None]
            wsl_war = [None]
            mwar = [None, None]
            qk_war = [[None, None], [None, None]]
            vv_war = [None, None]

            def emit_pair_loads(i, buf):
                w = (wsl_war[0],) if wsl_war[0] else ()
                for wi, (nm, dst) in enumerate((("wq", wqs), ("wk", wks), ("wv", wvs))):
                    sdma(f"w{nm[1]}s",
                         dst[:, :].rearrange("p (kc n) -> p kc n", kc=8),
                         WS[WIDX[nm], :, :].rearrange("p (kc m n) -> p kc m n", kc=8, m=8)
                         [:, :, i, :],
                         waits=w if wi == 0 else ())
                return (("wqs", p.val("wqs")), ("wks", p.val("wks")),
                        ("wvs", p.val("wvs")))

            def mk_proj_groups(i, buf, ldm):
                atoms = []
                for part in range(2):
                    wsl = (wqs, wks)[part]
                    bias = qk_evac_bias["qk"[part]][:, i:i + 1]
                    for ncc in range(4):
                        for Kc in range(8):
                            def g(part=part, wsl=wsl, bias=bias, ncc=ncc, i=i,
                                  buf=buf, ldm=ldm, Kc=Kc):
                                un = ncc % 2
                                p.op("tensor",
                                     lambda e, wsl=wsl, Kc=Kc, ncc=ncc, un=un: e.matmul(
                                         ps1[:, 1024 + un * 512: 1024 + (un + 1) * 512],
                                         wsl[:, Kc * 128:(Kc + 1) * 128],
                                         xn_bf[:, Kc * 2048 + ncc * 512: Kc * 2048 + (ncc + 1) * 512],
                                         start=(Kc == 0), stop=(Kc == 7)),
                                     waits=((ldm if (part == 0 and ncc == 0 and Kc == 0) else ()) +
                                            ((mwar[un],) if (mwar[un] and Kc == 0) else ())),
                                     inc=PE if Kc == 7 else None)
                                if Kc != 7:
                                    return
                                pm = p.val(PE)
                                proj_pe_last[0] = (PE, pm)
                                if part == 1 and ncc == 3:
                                    wsl_war[0] = (PE, pm)
                                ev = p.op("vector",
                                          lambda e, part=part, bias=bias, ncc=ncc, buf=buf, un=un: e.tensor_scalar(
                                              qk_t[part][buf][:, ncc * 512:(ncc + 1) * 512],
                                              ps1[:, 1024 + un * 512: 1024 + (un + 1) * 512],
                                              bias, None, ALU.add),
                                          waits=((PE, pm),) +
                                                ((qk_war[part][buf],) if (qk_war[part][buf] and ncc == 0) else ()),
                                          inc=DV)
                                mwar[un] = (DV, ev[1])
                                if ncc == 3:
                                    qk_war[part][buf] = (DV, ev[1])
                            atoms.append((g, Kc == 0))
                for tc in range(16):
                    for Kc in range(8):
                        def g(tc=tc, i=i, buf=buf, Kc=Kc):
                            un = tc % 2
                            p.op("tensor",
                                 lambda e, tc=tc, Kc=Kc, i=i, un=un: e.matmul(
                                     ps1[:, 1024 + un * 512: 1024 + un * 512 + 128],
                                     xn_bf[:, Kc * 2048 + tc * 128: Kc * 2048 + tc * 128 + 128],
                                     wvs[:, Kc * 128:(Kc + 1) * 128],
                                     start=(Kc == 0), stop=(Kc == 7)),
                                 waits=((mwar[un],) if (mwar[un] and Kc == 0) else ()),
                                 inc=PE if Kc == 7 else None)
                            if Kc != 7:
                                return
                            pm = p.val(PE)
                            proj_pe_last[0] = (PE, pm)
                            if tc == 15:
                                wsl_war[0] = (PE, pm)
                            ev = p.op("vector",
                                      lambda e, tc=tc, i=i, un=un: e.tensor_add(
                                          v_bf[:, tc * 1040 + (2 * i) * 65: tc * 1040 + (2 * i) * 65 + 130]
                                          .rearrange("p (h w) -> p h w", h=2)[:, :, 0:64],
                                          ps1[:, 1024 + un * 512: 1024 + un * 512 + 128]
                                          .rearrange("p (h w) -> p h w", h=2),
                                          bv_bcast[:, (2 * i) * 64: (2 * i) * 64 + 128]
                                          .rearrange("p (h w) -> p h w", h=2)),
                                      waits=((PE, pm), (DV, vones[1])), inc=DV)
                            mwar[un] = (DV, ev[1])
                            if tc == 15:
                                vv_war[buf] = (DV, ev[1])
                        atoms.append((g, Kc == 0))
                return atoms

            s_war = [None, None]
            psb_war = [None, None, None]
            ctx_war = [None, None]
            sums_war = [None]
            pending = []
            unit_no = [0]
            ppop = [0]
            pgoal = [24]
            if seq == 0:
                for wi, (nm, dst) in enumerate((("wq", wqs), ("wk", wks), ("wv", wvs))):
                    sdma(f"w{nm[1]}s",
                         dst[:, :].rearrange("p (kc n) -> p kc n", kc=8),
                         WS[WIDX[nm], :, :].rearrange("p (kc m n) -> p kc m n", kc=8, m=8)
                         [:, :, 0, :],
                         waits=wfs_done if wi == 0 else ())
                ldm_pair0 = (("wqs", p.val("wqs")), ("wks", p.val("wks")),
                             ("wvs", p.val("wvs")))
            proj_q = mk_proj_groups(0, 0, ldm_pair0)
            for g, _ in proj_q:
                g()
            ldm_next = emit_pair_loads(1, 1)
            proj_q = mk_proj_groups(1, 1, ldm_next)
            for i in range(NH // 2):
                buf = i % 2
                unit_no[0] = 0
                ppop[0] = 0
                pgoal[0] = len(proj_q)
                for qt in range(4):
                    exp_m = [None] * 16
                    pv_m = [None] * 16

                    def emit_pv(kc, i=i, qt=qt, buf=buf):
                        for px in range(2):
                            h = 2 * i + px
                            p.op("tensor",
                                 lambda e, h=h, px=px, kc=kc: e.matmul(
                                     ps1[0:65, px * 512:(px + 1) * 512],
                                     v_bf[:, kc * 1040 + h * 65: kc * 1040 + h * 65 + 65],
                                     p_sb[kc % 3][:, px * 512:(px + 1) * 512],
                                     start=(kc == 0), stop=(kc == 15)),
                                 waits=(((AC, exp_m[kc][1]),) if px == 0 else ()) +
                                       ((ctx_war[px],) if (ctx_war[px] and kc == 0) else ()),
                                 inc=PE if px == 1 else None)
                        pv_m[kc] = (PE, p.val(PE))
                        psb_war[kc % 3] = pv_m[kc]

                    for kc in range(16):
                        b = kc % 2
                        for px in range(2):
                            hp = px * 64
                            p.op("tensor",
                                 lambda e, hp=hp, px=px, qt=qt, kc=kc, buf=buf, b=b: e.matmul(
                                     ps0[:, b * 1024 + px * 512: b * 1024 + (px + 1) * 512],
                                     qk_t[1][buf][hp:hp + 64, kc * 128:(kc + 1) * 128],
                                     qk_t[0][buf][hp:hp + 64, qt * 512:(qt + 1) * 512],
                                     start=True, stop=True),
                                 waits=((s_war[b],) if (s_war[b] and px == 0) else ()) +
                                       ((qk_war[0][buf], qk_war[1][buf], vv_war[buf])
                                        if (qt == 0 and kc == 0 and px == 0) else ()),
                                 inc=PE if px == 1 else None)
                        sm = p.val(PE)
                        if kc >= 2:
                            emit_pv(kc - 2)
                        unit_no[0] += 1
                        if pending and (kc % 8) == 4:
                            while proj_q and not proj_q[0][1]:
                                proj_q.pop(0)[0]()
                                ppop[0] += 1
                            pending.pop(0)()
                        ptarget = ((unit_no[0] + 1) * pgoal[0]) // 64
                        while ppop[0] < ptarget and proj_q:
                            for _ in range(4):
                                if proj_q:
                                    proj_q.pop(0)[0]()
                                    ppop[0] += 1
                        em = p.op("scalar",
                                  lambda e, b=b, kc=kc: e.activation(
                                      p_sb[kc % 3][:, :], ps0[:, b * 1024:(b + 1) * 1024],
                                      AF.Exp, scale=0.125),
                                  waits=((PE, sm),) +
                                        ((psb_war[kc % 3],) if psb_war[kc % 3] else ()),
                                  inc=AC)
                        exp_m[kc] = em
                        s_war[b] = (AC, em[1])
                    emit_pv(14)
                    emit_pv(15)
                    pv_pe = p.val(PE)
                    cv_m = [None, None]
                    for px in range(2):
                        hp = px * 64
                        cv1 = p.op("vector",
                                   lambda e, i=i, hp=hp, px=px, qt=qt: e.tensor_copy(
                                       ctxn[hp:hp + 64, i * 2048 + qt * 512:
                                            i * 2048 + (qt + 1) * 512],
                                       ps1[0:64, px * 512:(px + 1) * 512]),
                                   waits=((PE, pv_pe),), inc=DV)
                        cv_m[px] = cv1
                    cv2 = p.op("vector",
                               lambda e: e.tensor_copy(sums_row[0:1, 0:1024],
                                                       ps1[64:65, 0:1024]),
                               waits=((sums_war[0],) if sums_war[0] else ()),
                               inc=DV)
                    ctx_war[0] = ctx_war[1] = (DV, cv2[1])

                    def mk_norm(i=i, qt=qt, cv2=cv2, cv_m=cv_m):
                        def emit_norm():
                            p.op("scalar",
                                 lambda e: e.activation(sums_row[0:1, 0:1024],
                                                        sums_row[0:1, 0:1024], AF.Ln),
                                 waits=((DV, cv2[1]),), inc=AC)
                            rc1 = p.op("scalar",
                                       lambda e: e.activation(sums_row[0:1, 0:1024],
                                                              sums_row[0:1, 0:1024],
                                                              AF.Exp, scale=-1.0),
                                       inc=AC)
                            for px in range(2):
                                hp = px * 64
                                p.op("tensor",
                                     lambda e, px=px: e.matmul(
                                         ps1[0:64, 1024 + px * 512: 1024 + (px + 1) * 512],
                                         ones128_sb[0:1, 0:64],
                                         sums_row[0:1, px * 512:(px + 1) * 512],
                                         start=True, stop=True),
                                     waits=((AC, rc1[1]),) +
                                           ((mwar[px],) if mwar[px] else ()),
                                     inc=PE)
                                bc_pe = p.val(PE)
                                sums_war[0] = (PE, bc_pe)
                                st = p.op("vector",
                                          lambda e, i=i, hp=hp, px=px, qt=qt: e.tensor_tensor(
                                              ctxn[hp:hp + 64, i * 2048 + qt * 512:
                                                   i * 2048 + (qt + 1) * 512],
                                              ctxn[hp:hp + 64, i * 2048 + qt * 512:
                                                   i * 2048 + (qt + 1) * 512],
                                              ps1[0:64, 1024 + px * 512: 1024 + (px + 1) * 512],
                                              ALU.mult),
                                          waits=((PE, bc_pe), (DV, cv_m[px][1])), inc=DV)
                                mwar[px] = (DV, st[1])
                        return emit_norm

                    pending.append(mk_norm())
                if i + 2 < NH // 2:
                    assert not proj_q, "proj queue not drained"
                    ldm2 = emit_pair_loads(i + 2, buf)
                    proj_q.extend(mk_proj_groups(i + 2, buf, ldm2))
            for g in list(pending):
                g()
            pending.clear()
            for g, _ in list(proj_q):
                g()
            proj_q.clear()
            for Kc in range(8):
                sdma("qpl1", qp_sb[:, 8192 + Kc * 1024: 8192 + (Kc + 1) * 1024],
                     QPS[:, Kc * 2048: Kc * 2048 + 1024],
                     waits=(spill_done + (wl_m[("done", 1)],)) if Kc == 0 else ())
            qp0_done = ("qpl1", p.val("qpl1"))
            xnfree = wsl_war[0]
            kp_m = []
            for jc in range(8):
                sdma(f"kpl{jc}",
                     kp_sb[:, :].rearrange("p (kc t n) -> p kc t n", kc=8, t=8)
                     [:, :, jc, :],
                     KPS[:, :].rearrange("p (kc t n) -> p kc t n", kc=8, t=8)
                     [:, :, jc, :],
                     waits=(spill_done + (xnfree,)) if jc == 0 else ())
                kp_m.append((f"kpl{jc}", p.val(f"kpl{jc}")))
            p.barrier()
            for Kc in range(8):
                sdma("qpl0", qp_sb[:, Kc * 1024:(Kc + 1) * 1024],
                     QPS[:, Kc * 2048 + 1024:(Kc + 1) * 2048],
                     waits=spill_done if Kc == 0 else ())
            ql_qt = {0: qp0_done, 1: ("qpl0", p.val("qpl0"))}
            sp_war = [None, None]
            pp_war = [None] * 16
            out_war = [None, None]
            xt_war = [None, None]
            osb_war = [None, None]
            rcpb2_war = None
            for qt in range(2):
                ql_done = ql_qt[qt]
                qoff = (1 - qt % 2) * 8192
                exp2 = [None] * 16
                for ktc in range(16):
                    sb_ = ktc % 2
                    for Kc in range(8):
                        for ncc in range(2):
                            p.op("tensor",
                                 lambda e, ktc=ktc, Kc=Kc, ncc=ncc, sb_=sb_, qoff=qoff: e.matmul(
                                     ps0[:, sb_ * 1024 + ncc * 512: sb_ * 1024 + (ncc + 1) * 512],
                                     kp_sb[:, Kc * 2048 + ktc * 128: Kc * 2048 + ktc * 128 + 128],
                                     qp_sb[:, qoff + Kc * 1024 + ncc * 512:
                                           qoff + Kc * 1024 + (ncc + 1) * 512],
                                     start=(Kc == 0), stop=(Kc == 7)),
                                 waits=(((kp_m[ktc // 2],) if (qt == 0 and ktc % 2 == 0) else ()) +
                                        ((ql_done,) if ktc == 0 else ()) +
                                        ((sp_war[sb_],) if sp_war[sb_] else ())
                                        if (Kc == 0 and ncc == 0) else ()),
                                 inc=PE if (ncc == 1 and Kc == 7) else None)
                    sm = p.val(PE)
                    em = p.op("scalar",
                              lambda e, ktc=ktc, sb_=sb_: e.activation(
                                  pp_sb[:, ktc * 1024:(ktc + 1) * 1024],
                                  ps0[:, sb_ * 1024:(sb_ + 1) * 1024],
                                  AF.Exp, scale=1.0 / 32.0),
                              waits=((PE, sm),) + ((pp_war[ktc],) if pp_war[ktc] else ()),
                              inc=AC)
                    exp2[ktc] = em
                    sp_war[sb_] = (AC, em[1])
                for ktc in range(16):
                    for ncc in range(2):
                        p.op("tensor",
                             lambda e, ktc=ktc, ncc=ncc: e.matmul(
                                 ps1[0:1, ncc * 512:(ncc + 1) * 512], onescolb_sb[:, :],
                                 pp_sb[:, ktc * 1024 + ncc * 512: ktc * 1024 + (ncc + 1) * 512],
                                 start=(ktc == 0), stop=(ktc == 15)),
                             waits=(((AC, exp2[ktc][1]),) if ncc == 0 else ()) +
                                   ((rcpb2_war,) if (rcpb2_war and ktc == 0 and ncc == 0) else ()),
                             inc=PE if (ktc == 15 and ncc == 1) else None)
                sums_pe = p.val(PE)
                ln2 = p.op("scalar",
                           lambda e: e.activation(rcp2a[:, :], ps1[0:1, 0:1024], AF.Ln),
                           waits=((PE, sums_pe),), inc=AC)
                rc2 = p.op("scalar",
                           lambda e: e.activation(rcp2b[:, :], rcp2a[:, :],
                                                  AF.Exp, scale=-1.0),
                           inc=AC)
                for ncc in range(2):
                    p.op("tensor",
                         lambda e, ncc=ncc: e.matmul(
                             ps1[:, ncc * 512:(ncc + 1) * 512], ones128_sb[0:1, :],
                             rcp2b[0:1, ncc * 512:(ncc + 1) * 512],
                             start=True, stop=True),
                         waits=((AC, rc2[1]),), inc=PE if ncc == 1 else None)
                bc2_pe = p.val(PE)
                pn_ms = []
                for ktc in range(16):
                    pn = p.op("vector",
                              lambda e, ktc=ktc: e.tensor_tensor(
                                  pp_sb[:, ktc * 1024:(ktc + 1) * 1024],
                                  pp_sb[:, ktc * 1024:(ktc + 1) * 1024],
                                  ps1[:, 0:1024], ALU.mult),
                              waits=((PE, bc2_pe),) if ktc == 0 else (), inc=DV)
                    pn_ms.append(pn)
                pnorm_done = pn_ms[-1]
                rcpb2_war = (DV, pnorm_done[1])
                for Mc in range(8):
                    for Kc in range(8):
                        for ncc in range(2):
                            p.op("tensor",
                                 lambda e, Mc=Mc, Kc=Kc, ncc=ncc, qt=qt: e.matmul(
                                     ps1[:, 1024 + ncc * 512: 1024 + (ncc + 1) * 512],
                                     wstream[1][:, Kc * 1024 + Mc * 128: Kc * 1024 + Mc * 128 + 128],
                                     ctxn[:, Kc * 2048 + qt * 1024 + ncc * 512:
                                          Kc * 2048 + qt * 1024 + (ncc + 1) * 512],
                                     start=(Kc == 0), stop=False),
                                 waits=(((wsm_wo,) if (Mc == 0 and Kc == 0 and ncc == 0) else ()) +
                                        ((out_war[ncc],) if (out_war[ncc] and Kc == 0) else ())))
                    for ktc in range(16):
                        for ncc in range(2):
                            p.op("tensor",
                                 lambda e, Mc=Mc, ktc=ktc, ncc=ncc: e.matmul(
                                     ps1[:, 1024 + ncc * 512: 1024 + (ncc + 1) * 512],
                                     vp_bf[:, ktc * 1024 + Mc * 128: ktc * 1024 + Mc * 128 + 128],
                                     pp_sb[:, ktc * 1024 + ncc * 512: ktc * 1024 + (ncc + 1) * 512],
                                     start=False, stop=(ktc == 15)),
                                 waits=((DV, pn_ms[ktc][1]),) if (Mc == 0 and ncc == 0) else (),
                                 inc=PE if ktc == 15 else None)
                    pm_last = p.val(PE)
                    for ncc in range(2):
                        uoff = 1024 + ncc * 512
                        pm = pm_last - 1 + ncc
                        xb = ncc % 2
                        xt = sdma(f"xtl{xb}", x_tile[xb][:, :],
                                  X[seq, Mc * 128:(Mc + 1) * 128,
                                    qt * 1024 + ncc * 512: qt * 1024 + (ncc + 1) * 512],
                                  waits=(xt_war[xb],) if xt_war[xb] else ())
                        ob = ncc % 2
                        ev = p.op("vector",
                                  lambda e, Mc=Mc, uoff=uoff, xb=xb, ob=ob: e.scalar_tensor_tensor(
                                      out_sb[ob][:, :], ps1[:, uoff:uoff + 512],
                                      bcols_sb[:, 16 + Mc:17 + Mc], x_tile[xb][:, :],
                                      ALU.add, ALU.add),
                                  waits=((PE, pm), xt) +
                                        ((osb_war[ob],) if osb_war[ob] else ()),
                                  inc=DV)
                        out_war[ncc] = (DV, ev[1])
                        xt_war[xb] = (DV, ev[1])
                        dm = _dma("gpsimd" if ob == 0 else "scalar",
                                  f"outs{ob}",
                                  OUT[seq, Mc * 128:(Mc + 1) * 128,
                                      qt * 1024 + ncc * 512: qt * 1024 + (ncc + 1) * 512],
                                  out_sb[ob][:, :], waits=(ev,))
                        osb_war[ob] = dm
                mark = (PE, p.val(PE))
                for ktc in range(16):
                    pp_war[ktc] = mark
            p.barrier()

        p.op("gpsimd", None, waits=(("outs0", p.val("outs0")),
                                    ("outs1", p.val("outs1"))))
        assert len(p.order) <= 200, f"too many semaphores: {len(p.order)}"
        p.emit(nc, ctx)
    return nc


def _host_inputs(inputs):
    f32 = np.float32
    asf = lambda a: np.ascontiguousarray(np.asarray(a), dtype=f32)
    d = {}
    d["wq"] = asf(np.asarray(inputs["L0_attn_Wq"]).T)
    d["wk"] = asf(np.asarray(inputs["L0_attn_Wk"]).T)
    d["wv"] = asf(np.asarray(inputs["L0_attn_Wv"]).T)
    d["wo"] = asf(np.asarray(inputs["L0_attn_Wo"]).T)
    for f in "qkv":
        w1 = np.asarray(inputs[f"L0_ffn_{f}_w1"])
        w2 = np.asarray(inputs[f"L0_ffn_{f}_w2"])
        d["w1" + f] = asf(w1.transpose(1, 0, 2).reshape(H, H))
        d["w2" + f] = asf(w2.reshape(H, H))
        d["b2" + f] = asf(inputs[f"L0_ffn_{f}_b2"])
    nw = np.concatenate([np.asarray(inputs[f"L0_ffn_{f}_node_w"]) for f in "qkv"], axis=0)
    nwt = np.zeros((H, 96), f32)
    for fi in range(3):
        nwt[:, fi * 32: fi * 32 + 15] = nw[fi * 15:(fi + 1) * 15].T
    d["nwt"] = nwt
    nodebc = np.zeros((96, 1), f32)
    for fi, f in enumerate("qkv"):
        nb = np.asarray(inputs[f"L0_ffn_{f}_node_b"], f32)
        nodebc[fi * 32: fi * 32 + 15, 0] = nb
        nodebc[fi * 32 + 15, 0] = 1.0
    d["nodebc"] = nodebc
    bcols = np.zeros((128, 48), f32)
    bcols[:, 0:8] = np.asarray(inputs["L0_attn_bq"], f32).reshape(8, 128).T
    bcols[:, 8:16] = np.asarray(inputs["L0_attn_bk"], f32).reshape(8, 128).T
    bcols[:, 16:24] = np.asarray(inputs["L0_attn_bo"], f32).reshape(8, 128).T
    for i, f in enumerate("qkv"):
        b1 = np.asarray(inputs[f"L0_ffn_{f}_b1"], f32).reshape(-1)
        bcols[:, 24 + 8 * i: 32 + 8 * i] = b1.reshape(8, 128).T
    d["bcols"] = bcols
    d["bvr"] = asf(inputs["L0_attn_bv"]).reshape(1, H)
    return d


def kernel(**inputs):
    x = np.asarray(inputs["L0_attn_hidden_states"], np.float32)
    xt = np.ascontiguousarray(x.transpose(0, 2, 1))
    wd = _host_inputs(inputs)
    nc = build(NSEQ)
    in_maps = []
    for c in range(NCORES):
        m = {"x": np.ascontiguousarray(xt[c * NSEQ:(c + 1) * NSEQ])}
        m.update(wd)
        in_maps.append(m)
    res = run_bass_kernel_spmd(nc, in_maps, core_ids=list(range(NCORES)))
    out = np.concatenate([r["out"] for r in res.results], axis=0)
    return np.ascontiguousarray(out.transpose(0, 2, 1))
